# version 06
# baseline (speedup 1.0000x reference)
import numpy as np
import ml_dtypes
from contextlib import ExitStack

import concourse.bass as bass
import concourse.mybir as mybir
from concourse.bass_utils import run_bass_kernel_spmd
from concourse.bass import broadcast_tensor_aps

F32 = mybir.dt.float32
BF16 = mybir.dt.bfloat16
I32 = mybir.dt.int32
AF = mybir.ActivationFunctionType
ALU = mybir.AluOpType
AX = mybir.AxisListType
bfd = ml_dtypes.bfloat16

D = 2048
HID = 2560
II = 1408
ISH = 2816
NF = 8
TW = 1024
SUB = 512
NSUB = 4
KB_D, KB_H, KB_I, KB_S = 16, 20, 11, 22
RING_SLOT = 5120
LN_EPS = 1e-5


def _pack_lhsT_piece(W, m0, mc):
    kb = W.shape[0] // 128
    return np.ascontiguousarray(
        W[:, m0:m0 + mc].reshape(kb, 128, mc).transpose(1, 0, 2).reshape(128, kb * mc)
    ).astype(np.float32)


def wstream_plan(ins):
    plan = []

    def pieces(W, K, M, mp128):
        out = []
        step = mp128 * 128
        for m0 in range(0, M, step):
            mc = min(step, M - m0)
            out.append((K // 128, mc, None if W is None else _pack_lhsT_piece(W, m0, mc)))
        return out

    def addp(tag, plist):
        for (kb, mc, arr) in plist:
            plan.append((tag, kb, mc, arr))

    def raw(tag, arr, P, C):
        a = None if arr is None else np.ascontiguousarray(arr, dtype=np.float32).reshape(P, C)
        plan.append((tag, None, (P, C), a))

    none = ins is None
    g = (lambda k: None) if none else (lambda k: np.asarray(ins[k], np.float32))

    gw = None if none else np.ascontiguousarray(g("L0_ffn_gate_weight").T)
    raw("gwT", None if none else
        gw.reshape(KB_D, 128, 64).transpose(1, 0, 2).reshape(128, KB_D * 64), 128, KB_D * 64)

    Wr = None if none else g("L0_attn_Wr")
    Wa = None if none else g("L0_attn_Wa")
    Wi = None if none else g("L0_attn_Wi")
    Wl = None if none else g("L0_attn_Wl")
    Wo = None if none else g("L0_attn_Wo")
    for sub in range(NSUB):
        addp("Wr%d" % sub, pieces(Wr, D, HID, 2))
        pa = pieces(Wa, HID, HID, 2)
        pi = pieces(Wi, HID, HID, 2)
        for a, b in zip(pa, pi):
            plan.append(("Wa%d" % sub, a[0], a[1], a[2]))
            plan.append(("Wi%d" % sub, b[0], b[1], b[2]))
        if sub >= 2:
            addp("Wl%d" % sub, pieces(Wl, D, HID, 2))
            addp("Wo%d" % sub, pieces(Wo, HID, D, 2))

    for f in range(NF):
        raw("mu%d" % f, None if none else np.asarray(ins["L0_ffn_mask_up_W"], np.float32)[f].T, 8, D)
        qa = None if none else np.concatenate(
            [np.asarray(ins["L0_ffn_gate_qa"], np.float32)[f].reshape(64, D).T,
             np.asarray(ins["L0_ffn_up_qa"], np.float32)[f].reshape(64, D).T], axis=1)
        addp("qa%d" % f, pieces(qa, D, 128, 1))
        raw("qbg%d" % f, None if none else
            np.asarray(ins["L0_ffn_gate_qb"], np.float32)[f].transpose(0, 2, 1).reshape(64, II), 64, II)
        addp("gW%d" % f, pieces(None if none else np.asarray(ins["L0_ffn_gate_W"], np.float32)[f].T,
                                D, II, 2))
        raw("qbu%d" % f, None if none else
            np.asarray(ins["L0_ffn_up_qb"], np.float32)[f].transpose(0, 2, 1).reshape(64, II), 64, II)
        addp("uW%d" % f, pieces(None if none else np.asarray(ins["L0_ffn_up_W"], np.float32)[f].T,
                                D, II, 2))
        addp("dqa%d" % f, pieces(None if none else
                                 np.asarray(ins["L0_ffn_down_qa"], np.float32)[f].reshape(64, II).T,
                                 II, 64, 1))
        raw("qbd%d" % f, None if none else
            np.asarray(ins["L0_ffn_down_qb"], np.float32)[f].transpose(0, 2, 1).reshape(64, D), 64, D)
        addp("dW%d" % f, pieces(None if none else np.asarray(ins["L0_ffn_down_W"], np.float32)[f].T,
                                II, D, 2))

    addp("shg", pieces(None if none else g("L0_ffn_sh_gate_W").T, D, ISH, 2))
    addp("shu", pieces(None if none else g("L0_ffn_sh_up_W").T, D, ISH, 2))
    addp("shd", pieces(None if none else g("L0_ffn_sh_down_W").T, ISH, D, 1))
    return plan


def host_vecs(ins):
    v = np.zeros((128, 272), np.float32)

    def put(col, vec, nt):
        v[:, col:col + nt] = np.asarray(vec, np.float32).reshape(nt, 128).T

    put(0, ins["L0_attn_br"], KB_H)
    put(20, ins["L0_attn_bl"], KB_H)
    put(40, ins["L0_attn_ba"], KB_H)
    put(60, ins["L0_attn_bi"], KB_H)
    put(80, ins["L0_attn_conv_b"], KB_H)
    put(100, ins["L0_attn_lam"], KB_H)
    put(120, ins["L0_attn_bo"], KB_D)
    cw = np.asarray(ins["L0_attn_conv_w"], np.float32).reshape(4, HID)
    for ci in range(KB_H):
        for j in range(4):
            v[:, 136 + ci * 4 + j] = cw[j, ci * 128:(ci + 1) * 128]
    return v


def host_cbf():
    c = np.zeros((128, 1292), np.float32)
    c[:, 0:128] = np.eye(128)
    for f in range(NF):
        for j in range(128):
            c[f * 8 + ((j % 64) // 8), 128 + f * 128 + j] = 1.0
    for fk in range(64):
        c[fk, 1152 + fk // 8] = 1.0
    c[:, 1160] = 1.0
    c[0, 1160:1288] = 1.0
    return c.astype(bfd)


def host_cf32():
    c = np.zeros((128, 262), np.float32)
    c[:, 0:128] = np.eye(128)
    c[0:64, 128] = np.arange(64)
    c[0, 130:258] = 1.0
    return c



class Sem:
    __slots__ = ("h", "n")

    def __init__(self, h):
        self.h = h
        self.n = 0


class PB:
    __slots__ = ("t", "rdy", "done", "w", "r")

    def __init__(self, t, rdy, done):
        self.t = t
        self.rdy = rdy
        self.done = done
        self.w = 0
        self.r = 0


def tokmax(*toks):
    best = {}
    stack = list(toks)
    while stack:
        t = stack.pop()
        if t is None:
            continue
        if isinstance(t, list):
            stack.extend(t)
            continue
        s, v = t
        cur = best.get(id(s))
        if cur is None or cur[1] < v:
            best[id(s)] = (s, v)
    return list(best.values())


class K:
    ENGS = ("sync", "scalar", "vector", "tensor", "gpsimd")

    YS_RS = (0, 516)
    YS_UACC = 1032
    YS_R = 1544
    YS_I = 2056
    YS_A = 2568
    YS_E2 = 3080
    YS_DV = 3592
    YS_SC = 4104
    YS_T5 = 1544
    RW_MEAN = 4616
    RW_RSTD = 5640
    RW_MS = 6664
    RW_VAR = 7688

    def __init__(self, taps=()):
        self.taps = list(taps)
        self.nc = bass.Bass("TRN2", target_bir_lowering=False)
        self.ops = {e: [] for e in self.ENGS}
        self.lastw = {e: {} for e in self.ENGS}
        self.plan = wstream_plan(None)
        self.plan_i = 0
        self.ws_cursor = 0
        self.ring_piece = 0
        self.ring_marks = []
        self.tap_outs = {}

    def emit(self, eng, fn, waits=(), inc=None, inc_by=1):
        lw = self.lastw[eng]
        real = []
        for s, v in (tokmax(*waits) if waits else []):
            if v > 0 and lw.get(id(s), 0) < v:
                lw[id(s)] = v
                real.append((s, v))
        tok = None
        if inc is not None:
            inc.n += inc_by
            tok = (inc, inc.n)

        def op(e, fn=fn, real=tuple(real), inc=inc, inc_by=inc_by):
            for s, v in real:
                e.wait_ge(s.h, v)
            inst = fn(e)
            if inc is not None:
                inst.then_inc(inc.h, inc_by)
        self.ops[eng].append(op)
        return tok

    def mm(self, pb, fns, waits=(), begin=True, end=True):
        if begin:
            pb.w += 1
            w = tokmax((pb.done, pb.r), *waits)
        else:
            w = tokmax(*waits)
        tok = None
        for i, fn in enumerate(fns):
            last = end and (i == len(fns) - 1)
            t = self.emit("tensor", fn, waits=w if i == 0 else (),
                          inc=pb.rdy if last else None)
            if last:
                tok = t
        return tok

    def rd(self, pb, eng, fn, waits=(), final=True, inc=None):
        w = tokmax((pb.rdy, pb.w), *waits)
        if final:
            pb.r += 1
            return self.emit(eng, fn, waits=w, inc=pb.done)
        return self.emit(eng, fn, waits=w, inc=inc)

    def declare(self):
        nc = self.nc
        total = sum((kb * 128 * mc if kb is not None else mc[0] * mc[1])
                    for (_, kb, mc, _) in self.plan)
        self.ws_total = total
        self.d_ws = nc.dram_tensor("wstream", [total], F32, kind="ExternalInput").ap()
        self.d_xt = nc.dram_tensor("xt", [D, 2048], F32, kind="ExternalInput").ap()
        self.d_vec = nc.dram_tensor("vecs", [128, 272], F32, kind="ExternalInput").ap()
        self.d_cbf = nc.dram_tensor("cbf", [128, 1292], BF16, kind="ExternalInput").ap()
        self.d_cf32 = nc.dram_tensor("cf32", [128, 262], F32, kind="ExternalInput").ap()
        self.d_inv = nc.dram_tensor("invm", [1, 64], I32, kind="ExternalInput").ap()
        self.d_hmask = nc.dram_tensor("hmask", [128, 1], F32, kind="ExternalInput").ap()
        self.d_gsc = nc.dram_tensor("gsc", [NF, II], F32, kind="ExternalInput").ap()
        self.d_usc = nc.dram_tensor("usc", [NF, II], F32, kind="ExternalInput").ap()
        self.d_dsc = nc.dram_tensor("dsc", [NF, D], F32, kind="ExternalInput").ap()
        self.d_out = nc.dram_tensor("out", [D, TW], F32, kind="ExternalOutput").ap()
        for name, shape in self.taps:
            self.tap_outs[name] = nc.dram_tensor(
                "tap_" + name, list(shape), F32, kind="ExternalOutput").ap()

    def alloc(self, cx):
        nc = self.nc
        sb = lambda n, s, d: cx.enter_context(nc.sbuf_tensor(n, s, d))
        self.RING = sb("RING", [128, 2 * RING_SLOT], BF16)
        self.ARA = sb("ARA", [128, 28672], BF16)
        self.RES = sb("RES", [128, 16384], BF16)
        self.CBF = sb("CBF", [128, 1292], BF16)
        self.MUT = sb("MUT", [64, 2048], BF16)
        self.SOFTC = sb("SOFTC", [64, 1024], BF16)
        self.WDC = sb("WDC", [64, 1024], BF16)
        self.GWT = sb("GWT", [128, KB_D * 64], BF16)
        self.PBF = sb("PBF", [64, 64], BF16)
        self.PTBF = sb("PTBF", [64, 64], BF16)
        self.PGBF = sb("PGBF", [64, 8], BF16)
        self.QROW = sb("QROW", [1, 2048], BF16)
        self.QB = sb("QB", [64, 2048], BF16)
        self.SCB = sb("SCB", [128, 1024], BF16)
        self.AW = sb("AW", [64, 2048], BF16)
        self.TTB = sb("TTB", [128, 1024], BF16)
        self.YS = sb("YS", [128, 16384], F32)
        self.VEC = sb("VEC", [128, 272], F32)
        self.HST = sb("HST", [128, 128], F32)
        self.GT = sb("GT", [128, 256], F32)
        self.CF32 = sb("CF32", [128, 262], F32)
        self.INV = sb("INV", [1, 64], I32)
        self.INVF = sb("INVF", [1, 64], F32)
        self.P32 = sb("P32", [64, 64], F32)

        mkps = lambda n, w: cx.enter_context(nc.psum_tensor(n, [128, w], F32))
        mks = lambda n: Sem(cx.enter_context(nc.semaphore(n)))
        self.psa = [PB(mkps("PA%d" % i, 512), mks("par%d" % i), mks("pad%d" % i))
                    for i in range(4)]
        self.ps = [PB(mkps("PF%d" % i, 1024), mks("pfr%d" % i), mks("pfd%d" % i))
                   for i in range(2)]
        self.ps = [self.ps[0], self.ps[1], self.ps[0], self.ps[1]]
        self.LD = mks("LD")
        self.GD = mks("GD")
        self.RG = [mks("RG0"), mks("RG1")]
        self.XS = mks("XS")
        self.MU = mks("MU")
        self.QBS = mks("QBS")
        self.GW = mks("GW")
        self.OUTS = [mks("OUT0"), mks("OUT1")]
        self.RF = mks("RF")
        self.VE = mks("VE")
        self.AC = mks("AC")
        self.PN = mks("PN")
        self.OUT = mks("OUT")

    def ws_next(self, tag_expect, dest=None, dest_part=0, waits=(), sem=None):
        tag, kb, mc, _ = self.plan[self.plan_i]
        assert tag == tag_expect, (tag, tag_expect, self.plan_i)
        self.plan_i += 1
        if kb is not None:
            nelem = kb * 128 * mc
            src = self.d_ws[self.ws_cursor:self.ws_cursor + nelem].rearrange("(p c) -> p c", p=128)
            piece = self.ring_piece
            base = (piece % 2) * RING_SLOT
            dst = self.RING[:, base:base + kb * mc]
            w = list(waits)
            if piece >= 2:
                assert len(self.ring_marks) >= piece - 1, (piece, len(self.ring_marks))
                w.append(self.ring_marks[piece - 2])
            tok = self.emit("gpsimd", lambda e, dst=dst, src=src: e.dma_start(out=dst, in_=src),
                            waits=w, inc=self.RG[piece % 2], inc_by=16)
            self.ring_piece += 1
            self.ws_cursor += nelem
            return base, kb, mc, tok
        else:
            P_, C_ = mc
            nelem = P_ * C_
            src = self.d_ws[self.ws_cursor:self.ws_cursor + nelem].rearrange("(p c) -> p c", p=P_)
            dst = dest[dest_part:dest_part + P_, 0:C_]
            tok = self.emit("gpsimd", lambda e, dst=dst, src=src: e.dma_start(out=dst, in_=src),
                            waits=waits, inc=sem if sem is not None else self.GD, inc_by=16)
            self.ws_cursor += nelem
            return None, P_, C_, tok

    def ring_free(self, tok):
        self.ring_marks.append(tok)

    def c_xs(self, kb):
        return kb * SUB

    def c_u(self, ci):
        return 8192 + ci * SUB

    def c_rt(self, ci):
        return 18432 + ci * SUB

    def c_xe(self, m):
        return m * TW

    def c_gate(self, m):
        return 16384 + m * TW

    def c_sh(self, ms):
        return 16384 + ms * TW if ms < KB_I else (ms - KB_I) * TW

    def prep(self):
        E = self.emit
        t_vec = E("sync", lambda e: e.dma_start(out=self.VEC[:], in_=self.d_vec),
                  inc=self.LD, inc_by=16)
        t_cbf = E("sync", lambda e: e.dma_start(out=self.CBF[:], in_=self.d_cbf),
                  inc=self.LD, inc_by=16)
        t_cf = E("sync", lambda e: e.dma_start(out=self.CF32[:], in_=self.d_cf32),
                 inc=self.LD, inc_by=16)
        t_inv = E("sync", lambda e: e.dma_start(out=self.INV[:], in_=self.d_inv),
                  inc=self.LD, inc_by=16)
        t_hm = E("sync", lambda e: e.dma_start(out=self.HST[:, 104:105], in_=self.d_hmask),
                 inc=self.LD, inc_by=16)
        self.t_consts = tokmax(t_vec, t_cbf, t_cf, t_inv, t_hm)

        _, _, _, self.t_gwt = self.ws_next("gwT", dest=self.GWT, sem=self.GW)

        t_m1 = E("vector", lambda e: e.memset(self.HST[:, 0:104], 0.0), inc=self.VE)
        t_m3 = E("vector", lambda e: e.memset(self.HST[:, 100:101], LN_EPS),
                 waits=[t_m1], inc=self.VE)
        self.t_memset = tokmax(t_m1, t_m3)

        t1 = E("scalar", lambda e: e.activation(
            self.YS[:, 0:20], self.VEC[:, 100:120], AF.Exp, scale=-1.0),
            waits=[*self.t_consts], inc=self.AC)
        t2 = E("scalar", lambda e: e.activation(
            self.YS[:, 20:40], self.YS[:, 0:20], AF.Ln, bias=1.0, scale=1.0),
            waits=[t1], inc=self.AC)
        self.t_n8 = E("vector", lambda e: e.tensor_scalar(
            self.VEC[:, 216:236], self.YS[:, 20:40], -8.0, None, ALU.mult),
            waits=[t2], inc=self.VE)

        t_if = E("vector", lambda e: e.tensor_copy(self.INVF[:], self.INV[:]),
                 waits=[*self.t_consts], inc=self.VE)
        pb = self.ps[2]
        self.mm(pb, [lambda e: e.matmul(pb.t[0:64, 0:64], self.CF32[0:1, 130:194],
                                        self.INVF[0:1, :], start=True, stop=True)],
                waits=[t_if, *self.t_memset])
        t_p = self.rd(pb, "vector", lambda e: e.tensor_scalar(
            self.PBF[:], pb.t[0:64, 0:64], self.CF32[0:64, 128:129], None, ALU.is_equal),
            final=False, inc=self.VE)
        t_p32 = self.rd(pb, "vector", lambda e: e.tensor_scalar(
            self.P32[:], pb.t[0:64, 0:64], self.CF32[0:64, 128:129], None, ALU.is_equal),
            waits=[t_p], final=True)
        pb3 = self.ps[3]
        self.mm(pb3, [lambda e: e.transpose(pb3.t[0:64, 0:64], self.P32[:],
                                            self.CF32[0:64, 0:64])], waits=[t_p32])
        t_pt = self.rd(pb3, "vector", lambda e: e.tensor_copy(self.PTBF[:], pb3.t[0:64, 0:64]))
        self.mm(pb3, [lambda e: e.matmul(pb3.t[0:64, 0:8], self.PTBF[:],
                                         self.CBF[0:64, 1152:1160], start=True, stop=True)],
                waits=[t_pt, *self.t_consts])
        self.t_pg = self.rd(pb3, "vector", lambda e: e.tensor_copy(self.PGBF[:], pb3.t[0:64, 0:8]))
        self.t_pbf = t_p

    def load_xs(self, sub, wait_toks):
        c0 = sub * SUB
        src = self.d_xt.rearrange("(kb p) t -> p kb t", p=128)[:, :, c0:c0 + SUB]
        dst = self.ARA[:, 0:8192].rearrange("p (kb t) -> p kb t", t=SUB)
        return self.emit("gpsimd", lambda e: e.dma_start(out=dst, in_=src),
                         waits=wait_toks, inc=self.XS, inc_by=16)

    def a1(self, sub, t_xs, t_aux):
        u_toks = [None] * KB_H
        chain_hist = [None, None]
        t_grp_last = None
        ci = 0
        for p in range((HID + 255) // 256):
            base, kb, mc, t_ld = self.ws_next("Wr%d" % sub)
            t_grp = None
            for mi in range(mc // 128):
                pb = self.psa[ci % 4]
                fns = []
                for k in range(KB_D):
                    def f(e, k=k, pb=pb, base=base, mc=mc, mi=mi):
                        o = base + k * mc + mi * 128
                        return e.matmul(pb.t[:, 0:SUB], self.RING[:, o:o + 128],
                                        self.ARA[:, self.c_xs(k):self.c_xs(k) + SUB],
                                        start=(k == 0), stop=(k == KB_D - 1))
                    fns.append(f)
                t_grp = self.mm(pb, fns, waits=[t_ld, t_xs])
                t_grp_last = t_grp
                off = (ci % 2) * 8712
                rs = off + 0
                t_ev = self.rd(pb, "scalar", lambda e, rs=rs, ci=ci, pb=pb: e.activation(
                    self.YS[:, rs + 3: rs + 3 + SUB], pb.t[:, 0:SUB], AF.Identity,
                    bias=self.VEC[:, ci:ci + 1], scale=1.0),
                    waits=tokmax(self.t_n8, *(t_aux if ci == 0 else [])), final=True)
                ch = self.conv_chain(ci, rs, off, t_ev, chain_hist[ci % 2],
                                     t_aux if ci <= 1 else [])
                u_toks[ci] = ch
                chain_hist[ci % 2] = ch
                ci += 1
            self.ring_free(t_grp)
        assert ci == KB_H
        return u_toks, t_grp_last

    def conv_chain(self, ci, rs, off, t_ev, t_prev, t_aux):
        E = self.emit
        UA = self.YS_UACC + off
        t_h = E("vector", lambda e, rs=rs, ci=ci: e.tensor_copy(
            self.YS[:, rs:rs + 3], self.HST[:, ci * 3:ci * 3 + 3]),
            waits=tokmax(t_ev, t_prev, *t_aux), inc=self.VE)
        t0 = E("scalar", lambda e, rs=rs, ci=ci, UA=UA: e.activation(
            self.YS[:, UA:UA + SUB], self.YS[:, rs:rs + SUB],
            AF.Identity, bias=self.VEC[:, 80 + ci:81 + ci],
            scale=self.VEC[:, 136 + ci * 4:137 + ci * 4]),
            waits=[t_h], inc=self.AC)
        t1 = E("vector", lambda e, rs=rs, ci=ci, UA=UA: e.scalar_tensor_tensor(
            self.YS[:, UA:UA + SUB], self.YS[:, rs + 1:rs + 1 + SUB],
            self.VEC[:, 137 + ci * 4:138 + ci * 4], self.YS[:, UA:UA + SUB],
            ALU.mult, ALU.add), waits=[t0], inc=self.VE)
        t2 = E("vector", lambda e, rs=rs, ci=ci, UA=UA: e.scalar_tensor_tensor(
            self.YS[:, UA:UA + SUB], self.YS[:, rs + 2:rs + 2 + SUB],
            self.VEC[:, 138 + ci * 4:139 + ci * 4], self.YS[:, UA:UA + SUB],
            ALU.mult, ALU.add), waits=[t1], inc=self.VE)
        t_sv = E("vector", lambda e, rs=rs, ci=ci: e.tensor_copy(
            self.HST[:, ci * 3:ci * 3 + 3], self.YS[:, rs + SUB:rs + SUB + 3]),
            waits=[t2], inc=self.VE)
        t3 = E("vector", lambda e, rs=rs, ci=ci, UA=UA: e.scalar_tensor_tensor(
            self.ARA[:, self.c_u(ci):self.c_u(ci) + SUB], self.YS[:, rs + 3:rs + 3 + SUB],
            self.VEC[:, 139 + ci * 4:140 + ci * 4], self.YS[:, UA:UA + SUB],
            ALU.mult, ALU.add), waits=[t_sv], inc=self.VE)
        return t3

    def a34(self, sub, u_toks, t_state0):
        E = self.emit
        t_u_all = tokmax(*u_toks)
        chain_hist = [t_state0, t_state0]
        t_chain = t_state0
        right_toks = [None] * KB_H
        for p in range((HID + 255) // 256):
            base_a, kba, mca, t_lda = self.ws_next("Wa%d" % sub)
            base_i, kbi, mci, t_ldi = self.ws_next("Wi%d" % sub)
            t_grp_a = t_grp_i = None
            for mi in range(mca // 128):
                ci = p * 2 + mi
                pa = self.psa[(2 * ci) % 4]
                pi_ = self.psa[(2 * ci + 1) % 4]
                off = (ci % 2) * 8712
                R_, I_, A_, E2, DV, SC = (self.YS_R + off, self.YS_I + off,
                                          self.YS_A + off, self.YS_E2 + off,
                                          self.YS_DV + off, self.YS_SC + off)
                fa = []
                fi = []
                for k in range(KB_H):
                    def f1(e, k=k, base=base_a, mc=mca, mi=mi, pb=pa):
                        o = base + k * mc + mi * 128
                        return e.matmul(pb.t[:, 0:SUB], self.RING[:, o:o + 128],
                                        self.ARA[:, self.c_u(k):self.c_u(k) + SUB],
                                        start=(k == 0), stop=(k == KB_H - 1))

                    def f2(e, k=k, base=base_i, mc=mci, mi=mi, pb=pi_):
                        o = base + k * mc + mi * 128
                        return e.matmul(pb.t[:, 0:SUB], self.RING[:, o:o + 128],
                                        self.ARA[:, self.c_u(k):self.c_u(k) + SUB],
                                        start=(k == 0), stop=(k == KB_H - 1))
                    fa.append(f1)
                    fi.append(f2)
                t_grp_a = self.mm(pa, fa, waits=[t_lda, t_u_all])
                t_grp_i = self.mm(pi_, fi, waits=[t_ldi, t_u_all])
                t_r = self.rd(pa, "scalar", lambda e, ci=ci, pb=pa, R_=R_: e.activation(
                    self.YS[:, R_:R_ + SUB], pb.t[:, 0:SUB], AF.Sigmoid,
                    bias=self.VEC[:, 40 + ci:41 + ci], scale=1.0),
                    waits=tokmax(chain_hist[ci % 2]), final=True)
                t_i = self.rd(pi_, "scalar", lambda e, ci=ci, pb=pi_, I_=I_: e.activation(
                    self.YS[:, I_:I_ + SUB], pb.t[:, 0:SUB], AF.Sigmoid,
                    bias=self.VEC[:, 60 + ci:61 + ci], scale=1.0),
                    waits=[t_r], final=True)
                tla = E("vector", lambda e, ci=ci, R_=R_: e.tensor_scalar(
                    self.YS[:, R_:R_ + SUB], self.YS[:, R_:R_ + SUB],
                    self.VEC[:, 216 + ci:217 + ci], None, ALU.mult),
                    waits=[t_r], inc=self.VE)
                ta = E("scalar", lambda e, R_=R_, A_=A_: e.activation(
                    self.YS[:, A_:A_ + SUB], self.YS[:, R_:R_ + SUB],
                    AF.Exp), waits=[tla], inc=self.AC)
                te2 = E("scalar", lambda e, R_=R_, E2=E2: e.activation(
                    self.YS[:, E2:E2 + SUB], self.YS[:, R_:R_ + SUB],
                    AF.Exp, scale=2.0), waits=[tla], inc=self.AC)
                tb2 = E("vector", lambda e, E2=E2: e.tensor_scalar(
                    self.YS[:, E2:E2 + SUB], self.YS[:, E2:E2 + SUB],
                    -1.0, 1.0, ALU.mult, ALU.add), waits=[te2], inc=self.VE)
                tb2b = E("vector", lambda e, E2=E2: e.tensor_scalar(
                    self.YS[:, E2:E2 + SUB], self.YS[:, E2:E2 + SUB],
                    1e-12, None, ALU.max), waits=[tb2], inc=self.VE)
                tbe = E("scalar", lambda e, E2=E2: e.activation(
                    self.YS[:, E2:E2 + SUB], self.YS[:, E2:E2 + SUB],
                    AF.Sqrt), waits=[tb2b], inc=self.AC)
                tdv = E("vector", lambda e, ci=ci, I_=I_, DV=DV: e.tensor_mul(
                    self.YS[:, DV:DV + SUB], self.YS[:, I_:I_ + SUB],
                    self.ARA[:, self.c_u(ci):self.c_u(ci) + SUB]), waits=[t_i], inc=self.VE)
                tdv2 = E("vector", lambda e, DV=DV, E2=E2: e.tensor_mul(
                    self.YS[:, DV:DV + SUB], self.YS[:, DV:DV + SUB],
                    self.YS[:, E2:E2 + SUB]), waits=[tbe, tdv], inc=self.VE)
                if sub >= 2:
                    tsc = E("vector", lambda e, ci=ci, A_=A_, DV=DV: e.tensor_tensor_scan(
                        self.ARA[:, self.c_rt(ci):self.c_rt(ci) + SUB],
                        self.YS[:, A_:A_ + SUB],
                        self.YS[:, DV:DV + SUB],
                        self.HST[:, 60 + ci:61 + ci], ALU.mult, ALU.add),
                        waits=[ta, tdv2], inc=self.VE)
                    tst = E("vector", lambda e, ci=ci: e.tensor_copy(
                        self.HST[:, 60 + ci:61 + ci],
                        self.ARA[:, self.c_rt(ci) + SUB - 1:self.c_rt(ci) + SUB]),
                        waits=[tsc], inc=self.VE)
                    right_toks[ci] = tsc
                else:
                    tsc = E("vector", lambda e, ci=ci, A_=A_, DV=DV, SC=SC: e.tensor_tensor_scan(
                        self.YS[:, SC:SC + SUB],
                        self.YS[:, A_:A_ + SUB],
                        self.YS[:, DV:DV + SUB],
                        self.HST[:, 60 + ci:61 + ci], ALU.mult, ALU.add),
                        waits=[ta, tdv2], inc=self.VE)
                    tst = E("vector", lambda e, ci=ci, SC=SC: e.tensor_copy(
                        self.HST[:, 60 + ci:61 + ci],
                        self.YS[:, SC + SUB - 1:SC + SUB]),
                        waits=[tsc], inc=self.VE)
                chain_hist[ci % 2] = tst
                t_chain = tst
            self.ring_free(t_grp_a)
            self.ring_free(t_grp_i)
        return right_toks, t_chain

    def mask_state(self, t_state):
        t1 = self.emit("vector", lambda e: e.tensor_scalar(
            self.HST[:, 60:80], self.HST[:, 60:80], self.HST[:, 104:105], None, ALU.mult),
            waits=tokmax(t_state, *self.t_consts), inc=self.VE)
        t2 = self.emit("vector", lambda e: e.tensor_scalar(
            self.HST[:, 0:60], self.HST[:, 0:60], self.HST[:, 104:105], None, ALU.mult),
            waits=[t1], inc=self.VE)
        return t2

    def a2(self, sub, right_toks, t_xs):
        E = self.emit
        lr_toks = [None] * KB_H
        prev_hist = [None, None]
        ci = 0
        for p in range((HID + 255) // 256):
            base, kb, mc, t_ld = self.ws_next("Wl%d" % sub)
            t_grp = None
            for mi in range(mc // 128):
                pb = self.psa[ci % 4]
                fns = []
                for k in range(KB_D):
                    def f(e, k=k, pb=pb, base=base, mc=mc, mi=mi):
                        o = base + k * mc + mi * 128
                        return e.matmul(pb.t[:, 0:SUB], self.RING[:, o:o + 128],
                                        self.ARA[:, self.c_xs(k):self.c_xs(k) + SUB],
                                        start=(k == 0), stop=(k == KB_D - 1))
                    fns.append(f)
                t_grp = self.mm(pb, fns, waits=[t_ld, t_xs])
                sl = (ci % 2) * SUB
                t_g = self.rd(pb, "scalar", lambda e, pb=pb, ci=ci, sl=sl: e.activation(
                    self.TTB[:, sl:sl + SUB], pb.t[:, 0:SUB], AF.Gelu_apprx_tanh,
                    bias=self.VEC[:, 20 + ci:21 + ci], scale=1.0),
                    waits=tokmax(prev_hist[ci % 2]), final=True)
                tlr = E("vector", lambda e, ci=ci, sl=sl: e.tensor_mul(
                    self.ARA[:, self.c_rt(ci):self.c_rt(ci) + SUB],
                    self.TTB[:, sl:sl + SUB],
                    self.ARA[:, self.c_rt(ci):self.c_rt(ci) + SUB]),
                    waits=tokmax(t_g, right_toks[ci]), inc=self.VE)
                lr_toks[ci] = tlr
                prev_hist[ci % 2] = tlr
                ci += 1
            self.ring_free(t_grp)
        return lr_toks

    def a5(self, sub, lr_toks, t_xs, t_slot_free):
        E = self.emit
        t_lr = tokmax(*lr_toks)
        wcol = (sub - 2) * SUB
        resid_toks = [None] * KB_D
        prev_hist = [t_slot_free, t_slot_free]
        m = 0
        s1, s2 = self.ps[2], self.ps[3]
        for p in range((D + 255) // 256):
            base, kb, mc, t_ld = self.ws_next("Wo%d" % sub)
            t_grp = None
            for mi in range(mc // 128):
                pb = self.psa[m % 4]
                T5 = self.YS_T5 + (m % 2) * 8712
                fns = []
                for k in range(KB_H):
                    def f(e, k=k, pb=pb, base=base, mc=mc, mi=mi):
                        o = base + k * mc + mi * 128
                        return e.matmul(pb.t[:, 0:SUB], self.RING[:, o:o + 128],
                                        self.ARA[:, self.c_rt(k):self.c_rt(k) + SUB],
                                        start=(k == 0), stop=(k == KB_H - 1))
                    fns.append(f)
                t_grp = self.mm(pb, fns, waits=[t_ld, *t_lr])
                t_e = self.rd(pb, "scalar", lambda e, pb=pb, m=m, T5=T5: e.activation(
                    self.YS[:, T5:T5 + SUB], pb.t[:, 0:SUB], AF.Identity,
                    bias=self.VEC[:, 120 + m:121 + m], scale=1.0),
                    waits=tokmax(prev_hist[m % 2]), final=True)
                rc = m * TW + wcol
                t_r = E("vector", lambda e, m=m, rc=rc, T5=T5: e.tensor_add(
                    self.RES[:, rc:rc + SUB], self.YS[:, T5:T5 + SUB],
                    self.ARA[:, self.c_xs(m):self.c_xs(m) + SUB]),
                    waits=tokmax(t_e, t_xs), inc=self.VE)
                resid_toks[m] = t_r
                sqc = 8192 + m * SUB
                t_sq = E("scalar", lambda e, rc=rc, sqc=sqc: e.activation(
                    self.ARA[:, sqc:sqc + SUB], self.RES[:, rc:rc + SUB], AF.Square),
                    waits=[t_r], inc=self.AC)
                prev_hist[m % 2] = t_sq
                first = (m == 0)
                if sub == 2 and m == 0:
                    s1.w += 1
                    s2.w += 1
                last = (sub == 3 and m == KB_D - 1)
                lastm = (m == KB_D - 1)
                self.emit("tensor", lambda e, rc=rc, wcol=wcol, first=first, lastm=lastm: e.matmul(
                    s1.t[0:1, wcol:wcol + SUB], self.CBF[:, 1160:1161],
                    self.RES[:, rc:rc + SUB], start=first, stop=lastm),
                    waits=tokmax(t_r, (s1.done, s1.r) if first else None),
                    inc=s1.rdy if last else None)
                inc2 = s2.rdy if last else (
                    self.PN if (sub == 2 and m == KB_D - 1) else None)
                self.emit("tensor", lambda e, sqc=sqc, wcol=wcol, first=first, lastm=lastm: e.matmul(
                    s2.t[0:1, wcol:wcol + SUB], self.CBF[:, 1160:1161],
                    self.ARA[:, sqc:sqc + SUB], start=first, stop=lastm),
                    waits=tokmax(t_sq, (s2.done, s2.r) if first else None),
                    inc=inc2)
                m += 1
            self.ring_free(t_grp)
        t_pe = (self.PN, self.PN.n) if sub == 2 else None
        return resid_toks, t_pe

    def ln_norm(self, src_fn, dst_fn, resid_all, dst_wait_fn=None):
        E = self.emit
        R = self.YS
        MEAN, RSTD, MS, VARC = self.RW_MEAN, self.RW_RSTD, self.RW_MS, self.RW_VAR
        s1, s2 = self.ps[2], self.ps[3]
        t_mean = self.rd(s1, "vector", lambda e: e.tensor_scalar(
            R[0:1, MEAN:MEAN + TW], s1.t[0:1, 0:TW], 1.0 / D, None, ALU.mult), final=True)
        t_ms = E("scalar", lambda e: e.activation(R[0:1, MS:MS + TW], R[0:1, MEAN:MEAN + TW],
                                                  AF.Square), waits=[t_mean], inc=self.AC)
        t_v1 = self.rd(s2, "vector", lambda e: e.tensor_scalar(
            R[0:1, VARC:VARC + TW], s2.t[0:1, 0:TW], 1.0 / D, None, ALU.mult), final=True)
        t_v2 = E("vector", lambda e: e.tensor_sub(
            R[0:1, VARC:VARC + TW], R[0:1, VARC:VARC + TW], R[0:1, MS:MS + TW]),
            waits=[t_ms, t_v1], inc=self.VE)
        t_sd = E("scalar", lambda e: e.activation(
            R[0:1, VARC:VARC + TW], R[0:1, VARC:VARC + TW], AF.Sqrt,
            bias=self.HST[0:1, 100:101], scale=1.0), waits=[t_v2], inc=self.AC)
        t_rst = E("vector", lambda e: e.reciprocal(
            R[0:1, RSTD:RSTD + TW], R[0:1, VARC:VARC + TW]), waits=[t_sd], inc=self.VE)
        pm, pr = self.ps[0], self.ps[1]
        self.mm(pm, [
            (lambda e, n0=n0: e.matmul(pm.t[:, n0:n0 + SUB], self.CF32[0:1, 130:258],
                                       R[0:1, MEAN + n0:MEAN + n0 + SUB], start=True, stop=True))
            for n0 in (0, SUB)], waits=[t_mean])
        self.mm(pr, [
            (lambda e, n0=n0: e.matmul(pr.t[:, n0:n0 + SUB], self.CF32[0:1, 130:258],
                                       R[0:1, RSTD + n0:RSTD + n0 + SUB], start=True, stop=True))
            for n0 in (0, SUB)], waits=[t_rst])
        out = []
        hist = [None, None]
        for m in range(KB_D):
            sl = 2048 + (m % 2) * 1024
            w1 = tokmax(*(resid_all if m == 0 else []), hist[m % 2])
            t_s = self.rd(pm, "vector", lambda e, m=m, sl=sl: e.tensor_sub(
                self.YS[:, sl:sl + TW], src_fn(m), pm.t[:, 0:TW]),
                waits=w1, final=True)
            w2 = [t_s] + ([dst_wait_fn(m)] if dst_wait_fn else [])
            t_n = self.rd(pr, "vector", lambda e, m=m, sl=sl: e.tensor_mul(
                dst_fn(m), self.YS[:, sl:sl + TW], pr.t[:, 0:TW]),
                waits=w2, final=True)
            hist[m % 2] = t_n
            out.append(t_n)
        return out

    def gating(self, x1_toks):
        E = self.emit
        G = self.GT
        t_x1 = tokmax(*x1_toks)
        t_wdc = [None] * 8
        t_sfc = [None] * 8
        p2, p3 = self.ps[2], self.ps[3]
        t_prev_g = None
        for tt in range(8):
            fns = []
            for k in range(KB_D):
                def f(e, k=k, tt=tt):
                    return e.matmul(p2.t[:, 0:64],
                                    self.RES[:, k * TW + tt * 128:k * TW + tt * 128 + 128],
                                    self.GWT[:, k * 64:(k + 1) * 64],
                                    start=(k == 0), stop=(k == KB_D - 1))
                fns.append(f)
            self.mm(p2, fns, waits=[*t_x1, self.t_gwt])
            t_nm = self.rd(p2, "vector", lambda e: e.reduce_max(
                G[:, 192:193], p2.t[:, 0:64], axis=AX.X, negate=True),
                waits=tokmax(t_prev_g), final=False, inc=self.VE)
            t_e = self.rd(p2, "scalar", lambda e: e.activation(
                G[:, 0:64], p2.t[:, 0:64], AF.Exp, bias=G[:, 192:193], scale=1.0,
                accum_out=G[:, 193:194]), waits=[t_nm], final=True)
            t_rc = E("vector", lambda e: e.reciprocal(G[:, 194:195], G[:, 193:194]),
                     waits=[t_e], inc=self.VE)
            t_sc = E("vector", lambda e: e.tensor_scalar(
                G[:, 0:64], G[:, 0:64], G[:, 194:195], None, ALU.mult),
                waits=[t_rc], inc=self.VE)
            t_w = E("vector", lambda e: e.tensor_copy(G[:, 64:128], G[:, 0:64]),
                    waits=[t_sc], inc=self.VE)
            for r in range(5):
                t_m = E("vector", lambda e: e.reduce_max(G[:, 195:196], G[:, 64:128], axis=AX.X),
                        waits=[t_w], inc=self.VE)
                t_k = E("vector", lambda e: e.tensor_scalar(
                    G[:, 128:192], G[:, 64:128], G[:, 195:196], -2e9, ALU.is_ge, ALU.mult),
                    waits=[t_m], inc=self.VE)
                t_w = E("vector", lambda e: e.tensor_add(G[:, 64:128], G[:, 64:128], G[:, 128:192]),
                        waits=[t_k], inc=self.VE)
            t_v6 = E("vector", lambda e: e.reduce_max(G[:, 195:196], G[:, 64:128], axis=AX.X),
                     waits=[t_w], inc=self.VE)
            t_ind = E("vector", lambda e: e.tensor_scalar(
                G[:, 128:192], G[:, 0:64], G[:, 195:196], None, ALU.is_ge),
                waits=[t_v6], inc=self.VE)
            t_wd = E("vector", lambda e: e.tensor_mul(G[:, 64:128], G[:, 128:192], G[:, 0:64]),
                     waits=[t_ind], inc=self.VE)
            self.mm(p3, [lambda e: e.transpose(p3.t[0:64, 0:128], G[:, 64:128],
                                               self.CF32[:, 0:128])], waits=[t_wd])
            t_wdc[tt] = self.rd(p3, "vector", lambda e, tt=tt: e.tensor_copy(
                self.WDC[:, tt * 128:(tt + 1) * 128], p3.t[0:64, 0:128]))
            self.mm(p2, [lambda e, tt=tt: e.matmul(
                p2.t[:, 0:64], self.WDC[:, tt * 128:(tt + 1) * 128], self.PBF[:],
                start=True, stop=True)], waits=[t_wdc[tt], self.t_pbf])
            t_z1 = self.rd(p2, "vector", lambda e: e.tensor_scalar(
                G[:, 128:192], p2.t[:, 0:64], 0.0, -1e9, ALU.is_equal, ALU.mult),
                final=False, inc=self.VE)
            t_z2 = self.rd(p2, "vector", lambda e: e.tensor_add(
                G[:, 128:192], G[:, 128:192], p2.t[:, 0:64]), waits=[t_z1], final=True)
            z3 = G[:, 128:192].rearrange("p (g k) -> p g k", g=8)
            gm3 = G[:, 224:232].rearrange("p (g k) -> p g k", k=1)
            t_gm = E("vector", lambda e, z3=z3: e.reduce_max(
                G[:, 224:232], z3, axis=AX.X, negate=True), waits=[t_z2], inc=self.VE)

            def f_zc(e, z3=z3, gm3=gm3):
                a, b = broadcast_tensor_aps(z3, gm3)
                return e.tensor_add(a, a, b)
            t_zc = E("vector", f_zc, waits=[t_gm], inc=self.VE)
            t_ez = E("scalar", lambda e: e.activation(G[:, 128:192], G[:, 128:192], AF.Exp),
                     waits=[t_zc], inc=self.AC)
            t_gs = E("vector", lambda e, z3=z3: e.reduce_sum(G[:, 232:240], z3, axis=AX.X),
                     waits=[t_ez], inc=self.VE)
            t_gr = E("vector", lambda e: e.reciprocal(G[:, 240:248], G[:, 232:240]),
                     waits=[t_gs], inc=self.VE)
            gr3 = G[:, 240:248].rearrange("p (g k) -> p g k", k=1)

            def f_sf(e, z3=z3, gr3=gr3):
                a, b = broadcast_tensor_aps(z3, gr3)
                return e.tensor_mul(a, a, b)
            t_sf = E("vector", f_sf, waits=[t_gr], inc=self.VE)
            self.mm(p3, [lambda e: e.transpose(p3.t[0:64, 0:128], G[:, 128:192],
                                               self.CF32[:, 0:128])], waits=[t_sf])
            t_sfc[tt] = self.rd(p3, "vector", lambda e, tt=tt: e.tensor_copy(
                self.SOFTC[:, tt * 128:(tt + 1) * 128], p3.t[0:64, 0:128]))
            t_prev_g = t_sfc[tt]
        self.t_wdc = tokmax(*t_wdc)
        self.t_sfc = tokmax(*t_sfc)

    def expert(self, f, t_free, t_mu_prev, t_y_prev):
        E = self.emit
        p2, p3 = self.ps[2], self.ps[3]
        self.mm(p2, [
            (lambda e, n0=n0: e.matmul(p2.t[0:64, n0:n0 + SUB],
                                       self.CBF[0:64, 128 + f * 128:128 + f * 128 + 64],
                                       self.SOFTC[0:64, n0:n0 + SUB], start=True, stop=True))
            for n0 in (0, SUB)], waits=[*self.t_sfc, *self.t_consts])
        t_w128 = self.rd(p2, "vector", lambda e: e.tensor_copy(
            self.ARA[0:64, 27648:28672], p2.t[0:64, 0:TW]), waits=tokmax(*(t_free or [])))
        self.mm(p3, [
            (lambda e, n0=n0: e.matmul(p3.t[0:1, n0:n0 + SUB], self.PGBF[0:64, f:f + 1],
                                       self.WDC[0:64, n0:n0 + SUB], start=True, stop=True))
            for n0 in (0, SUB)], waits=[*self.t_wdc, self.t_pg])
        t_scb = self.rd(p3, "vector", lambda e: e.tensor_copy(
            self.QROW[0:1, 1024:2048], p3.t[0:1, 0:TW]))
        self.mm(p3, [
            (lambda e, n0=n0: e.matmul(p3.t[:, n0:n0 + SUB], self.CBF[0:1, 1160:1288],
                                       self.QROW[0:1, 1024 + n0:1024 + n0 + SUB],
                                       start=True, stop=True))
            for n0 in (0, SUB)], waits=[t_scb])
        t_scbc = self.rd(p3, "vector", lambda e: e.tensor_copy(self.SCB[:], p3.t[:, 0:TW]))
        t_mz = E("vector", lambda e: e.memset(self.MUT[:], 0.0),
                 waits=tokmax(t_mu_prev), inc=self.VE)
        _, _, _, t_mu = self.ws_next("mu%d" % f, dest=self.MUT, dest_part=f * 8, waits=[t_mz], sem=self.MU)
        t_xe = [None] * KB_D
        for m in range(KB_D):
            pb = self.ps[m % 2]
            self.mm(pb, [
                (lambda e, n0=n0, m=m, pb=pb: e.matmul(
                    pb.t[:, n0:n0 + SUB], self.MUT[:, m * 128:m * 128 + 128],
                    self.SOFTC[0:64, n0:n0 + SUB], start=True, stop=True))
                for n0 in (0, SUB)], waits=[t_mu, *self.t_sfc])
            t_xe[m] = self.rd(pb, "vector", lambda e, m=m, pb=pb: e.tensor_add(
                self.ARA[:, self.c_xe(m):self.c_xe(m) + TW], pb.t[:, 0:TW],
                self.RES[:, m * TW:(m + 1) * TW]),
                waits=tokmax(*(t_free or [])) if m == 0 else ())
        t_xe_all = tokmax(*t_xe)
        base, kb, mc, t_ld = self.ws_next("qa%d" % f)
        fg = []
        fu = []
        for k in range(KB_D):
            for n0 in (0, SUB):
                def fq(e, k=k, n0=n0, base=base):
                    return e.matmul(p2.t[0:64, n0:n0 + SUB],
                                    self.RING[:, base + k * 128:base + k * 128 + 64],
                                    self.ARA[:, self.c_xe(k) + n0:self.c_xe(k) + n0 + SUB],
                                    start=(k == 0), stop=(k == KB_D - 1))
                fg.append(fq)

                def fq2(e, k=k, n0=n0, base=base):
                    return e.matmul(p3.t[0:64, n0:n0 + SUB],
                                    self.RING[:, base + k * 128 + 64:base + k * 128 + 128],
                                    self.ARA[:, self.c_xe(k) + n0:self.c_xe(k) + n0 + SUB],
                                    start=(k == 0), stop=(k == KB_D - 1))
                fu.append(fq2)
        self.mm(p2, fg, waits=[t_ld, *t_xe_all])
        t_qa_grp = self.mm(p3, fu, waits=[t_ld, *t_xe_all])
        self.ring_free(t_qa_grp)
        t_aw_g = self.rd(p2, "vector", lambda e: e.tensor_mul(
            self.AW[0:64, 0:TW], p2.t[0:64, 0:TW], self.ARA[0:64, 27648:28672]),
            waits=[t_w128])
        t_aw_u = self.rd(p3, "vector", lambda e: e.tensor_mul(
            self.AW[0:64, TW:2 * TW], p3.t[0:64, 0:TW], self.ARA[0:64, 27648:28672]),
            waits=[t_w128])
        t_aw = tokmax(t_aw_g, t_aw_u)
        t_qbg = self.qb_scale("qbg%d" % f, self.QB, II, self.d_gsc, f, t_free)
        t_gate = self.ffn_big("gW%d" % f, KB_D, KB_I, self.QB, t_qbg, t_aw, 0,
                              evac="silu", xe_all=t_xe_all, region_free=t_free)
        t_qbu = self.qb_scale("qbu%d" % f, self.QB, II, self.d_usc, f, t_gate)
        t_gu = self.ffn_big("uW%d" % f, KB_D, KB_I, self.QB, t_qbu, t_aw, TW,
                            evac="gu", xe_all=t_xe_all, gate_toks=t_gate, t_scbc=t_scbc)
        t_gu_all = tokmax(*t_gu)
        base, kb, mc, t_ld = self.ws_next("dqa%d" % f)
        fns = []
        for k in range(KB_I):
            for n0 in (0, SUB):
                def fd(e, k=k, n0=n0, base=base):
                    return e.matmul(p2.t[0:64, n0:n0 + SUB],
                                    self.RING[:, base + k * 64:base + k * 64 + 64],
                                    self.ARA[:, self.c_gate(k) + n0:self.c_gate(k) + n0 + SUB],
                                    start=(k == 0), stop=(k == KB_I - 1))
                fns.append(fd)
        t_dqa_grp = self.mm(p2, fns, waits=[t_ld, *t_gu_all])
        self.ring_free(t_dqa_grp)
        t_awd = self.rd(p2, "vector", lambda e: e.tensor_mul(
            self.AW[0:64, 0:TW], p2.t[0:64, 0:TW], self.ARA[0:64, 27648:28672]),
            waits=[t_w128, *t_aw])
        t_qbd = self.qb_scale("qbd%d" % f, self.QB, D, self.d_dsc, f, t_gu)
        t_y = self.ffn_down("dW%d" % f, f, t_qbd, t_awd, t_gu_all, t_y_prev)
        return t_xe_all, t_y

    def qb_scale(self, tag, dst, C, d_src, f, t_free):
        E = self.emit
        p3 = self.ps[3]
        _, _, _, t_ld = self.ws_next(tag, dest=dst, waits=tokmax(*(t_free or [])), sem=self.QBS)
        t_g = E("gpsimd", lambda e: e.dma_start(out=self.QROW[0:1, 0:C],
                                                in_=d_src[f:f + 1, :]),
                waits=[(p3.rdy, p3.w)], inc=self.QBS, inc_by=16)
        toks = []
        for c0 in range(0, C, TW):
            cw = min(TW, C - c0)
            fns = []
            for n0 in range(0, cw, SUB):
                nw = min(SUB, cw - n0)

                def fb(e, c0=c0, n0=n0, nw=nw):
                    return e.matmul(p3.t[0:64, n0:n0 + nw], self.CBF[0:1, 1160:1224],
                                    self.QROW[0:1, c0 + n0:c0 + n0 + nw], start=True, stop=True)
                fns.append(fb)
            self.mm(p3, fns, waits=[t_g])
            toks.append(self.rd(p3, "vector", lambda e, c0=c0, cw=cw, dst=dst: e.tensor_mul(
                dst[0:64, c0:c0 + cw], dst[0:64, c0:c0 + cw], p3.t[0:64, 0:cw]),
                waits=[t_ld, t_g]))
        return tokmax(*toks)

    def ffn_big(self, tag, kbn, mout, qb, t_qb, t_aw, aw_row, evac, xe_all,
                gate_toks=None, t_scbc=None, region_free=None):
        out_toks = [None] * mout
        m = 0
        t_tt_prev = None
        for p in range((mout * 128 + 255) // 256):
            base, kb, mc, t_ld = self.ws_next(tag)
            t_grp = None
            for mi in range(mc // 128):
                pb = self.ps[m % 2]
                fns = []
                for k in range(kbn):
                    for n0 in (0, SUB):
                        def f1(e, k=k, n0=n0, base=base, mc=mc, mi=mi, pb=pb):
                            o = base + k * mc + mi * 128
                            return e.matmul(pb.t[:, n0:n0 + SUB], self.RING[:, o:o + 128],
                                            self.ARA[:, self.c_xe(k) + n0:self.c_xe(k) + n0 + SUB],
                                            start=(k == 0), stop=False)
                        fns.append(f1)
                for n0 in (0, SUB):
                    def f2(e, n0=n0, m=m, pb=pb, qb=qb, aw_row=aw_row):
                        return e.matmul(pb.t[:, n0:n0 + SUB], qb[0:64, m * 128:m * 128 + 128],
                                        self.AW[0:64, aw_row + n0:aw_row + n0 + SUB],
                                        start=False, stop=True)
                    fns.append(f2)
                t_grp = self.mm(pb, fns, waits=[t_ld, *xe_all, t_qb, *tokmax(t_aw)])
                if evac == "silu":
                    out_toks[m] = self.rd(pb, "scalar", lambda e, m=m, pb=pb: e.activation(
                        self.ARA[:, self.c_gate(m):self.c_gate(m) + TW], pb.t[:, 0:TW], AF.Silu),
                        waits=tokmax(*(region_free or [])) if m == 0 else ())
                else:
                    t_t1 = self.rd(pb, "vector", lambda e, m=m, pb=pb: e.tensor_mul(
                        self.TTB[:], pb.t[:, 0:TW],
                        self.ARA[:, self.c_gate(m):self.c_gate(m) + TW]),
                        waits=tokmax(gate_toks[m], t_tt_prev))
                    t_gu = self.emit("vector", lambda e, m=m: e.tensor_mul(
                        self.ARA[:, self.c_gate(m):self.c_gate(m) + TW], self.TTB[:],
                        self.SCB[:]), waits=[t_t1, t_scbc], inc=self.VE)
                    t_tt_prev = t_gu
                    out_toks[m] = t_gu
                m += 1
            self.ring_free(t_grp)
        return out_toks

    def ffn_down(self, tag, f, t_qbd, t_awd, t_gu_all, t_y_prev):
        t_y = [None] * KB_D
        m = 0
        for p in range((D + 255) // 256):
            base, kb, mc, t_ld = self.ws_next(tag)
            t_grp = None
            for mi in range(mc // 128):
                pb = self.ps[m % 2]
                fns = []
                for k in range(KB_I):
                    for n0 in (0, SUB):
                        def f1(e, k=k, n0=n0, base=base, mc=mc, mi=mi, pb=pb):
                            o = base + k * mc + mi * 128
                            return e.matmul(pb.t[:, n0:n0 + SUB], self.RING[:, o:o + 128],
                                            self.ARA[:, self.c_gate(k) + n0:self.c_gate(k) + n0 + SUB],
                                            start=(k == 0), stop=False)
                        fns.append(f1)
                for n0 in (0, SUB):
                    def f2(e, n0=n0, m=m, pb=pb):
                        return e.matmul(pb.t[:, n0:n0 + SUB], self.QB[0:64, m * 128:m * 128 + 128],
                                        self.AW[0:64, n0:n0 + SUB], start=False, stop=True)
                    fns.append(f2)
                t_grp = self.mm(pb, fns, waits=[t_ld, *t_gu_all, t_qbd, t_awd])
                if f == 0:
                    t_y[m] = self.rd(pb, "vector", lambda e, m=m, pb=pb: e.tensor_copy(
                        self.YS[:, m * TW:(m + 1) * TW], pb.t[:, 0:TW]),
                        waits=tokmax(*(t_y_prev or [])) if m == 0 else ())
                else:
                    t_y[m] = self.rd(pb, "vector", lambda e, m=m, pb=pb: e.tensor_add(
                        self.YS[:, m * TW:(m + 1) * TW], self.YS[:, m * TW:(m + 1) * TW],
                        pb.t[:, 0:TW]),
                        waits=tokmax(*(t_y_prev or [])) if m == 0 else ())
                m += 1
            self.ring_free(t_grp)
        return t_y

    def shared(self, t_free, t_y_prev):
        E = self.emit
        t_shg = [None] * KB_S
        ms = 0
        for p in range((ISH + 255) // 256):
            base, kb, mc, t_ld = self.ws_next("shg")
            t_grp = None
            for mi in range(mc // 128):
                pb = self.ps[ms % 2]
                fns = []
                for k in range(KB_D):
                    for n0 in (0, SUB):
                        def f1(e, k=k, n0=n0, base=base, mc=mc, mi=mi, pb=pb):
                            o = base + k * mc + mi * 128
                            return e.matmul(pb.t[:, n0:n0 + SUB], self.RING[:, o:o + 128],
                                            self.RES[:, k * TW + n0:k * TW + n0 + SUB],
                                            start=(k == 0), stop=(k == KB_D - 1))
                        fns.append(f1)
                t_grp = self.mm(pb, fns, waits=[t_ld, *t_free])
                t_shg[ms] = self.rd(pb, "scalar", lambda e, ms=ms, pb=pb: e.activation(
                    self.ARA[:, self.c_sh(ms):self.c_sh(ms) + TW], pb.t[:, 0:TW], AF.Silu),
                    waits=tokmax(*(t_free if ms < 2 else [])))
                ms += 1
            self.ring_free(t_grp)
        t_prod = [None] * KB_S
        ms = 0
        for p in range((ISH + 255) // 256):
            base, kb, mc, t_ld = self.ws_next("shu")
            t_grp = None
            for mi in range(mc // 128):
                pb = self.ps[ms % 2]
                fns = []
                for k in range(KB_D):
                    for n0 in (0, SUB):
                        def f1(e, k=k, n0=n0, base=base, mc=mc, mi=mi, pb=pb):
                            o = base + k * mc + mi * 128
                            return e.matmul(pb.t[:, n0:n0 + SUB], self.RING[:, o:o + 128],
                                            self.RES[:, k * TW + n0:k * TW + n0 + SUB],
                                            start=(k == 0), stop=(k == KB_D - 1))
                        fns.append(f1)
                t_grp = self.mm(pb, fns, waits=[t_ld])
                t_prod[ms] = self.rd(pb, "vector", lambda e, ms=ms, pb=pb: e.tensor_mul(
                    self.ARA[:, self.c_sh(ms):self.c_sh(ms) + TW],
                    self.ARA[:, self.c_sh(ms):self.c_sh(ms) + TW], pb.t[:, 0:TW]),
                    waits=[t_shg[ms]])
                ms += 1
            self.ring_free(t_grp)
        t_prod_all = tokmax(*t_prod)
        t_y = [None] * KB_D
        m = 0
        for p in range((D + 127) // 128):
            base, kb, mc, t_ld = self.ws_next("shd")
            t_grp = None
            for mi in range(mc // 128):
                pb = self.ps[m % 2]
                fns = []
                for k in range(KB_S):
                    for n0 in (0, SUB):
                        def f1(e, k=k, n0=n0, base=base, mc=mc, mi=mi, pb=pb):
                            o = base + k * mc + mi * 128
                            return e.matmul(pb.t[:, n0:n0 + SUB], self.RING[:, o:o + 128],
                                            self.ARA[:, self.c_sh(k) + n0:self.c_sh(k) + n0 + SUB],
                                            start=(k == 0), stop=(k == KB_S - 1))
                        fns.append(f1)
                t_grp = self.mm(pb, fns, waits=[t_ld, *t_prod_all])
                t_y[m] = self.rd(pb, "vector", lambda e, m=m, pb=pb: e.tensor_add(
                    self.YS[:, m * TW:(m + 1) * TW], self.YS[:, m * TW:(m + 1) * TW],
                    pb.t[:, 0:TW]),
                    waits=tokmax(*(t_y_prev or [])) if m == 0 else ())
                m += 1
            self.ring_free(t_grp)
        return t_y

    def final(self, t_y):
        E = self.emit
        t_yall = tokmax(*t_y)
        s1, s2 = self.ps[2], self.ps[3]
        r2_toks = []
        t_sq_prev = None
        pn_toks = []
        s1.w += 1
        s2.w += 1
        for m in range(KB_D):
            t_r2 = E("vector", lambda e, m=m: e.tensor_add(
                self.ARA[:, m * TW:(m + 1) * TW], self.RES[:, m * TW:(m + 1) * TW],
                self.YS[:, m * TW:(m + 1) * TW]),
                waits=t_yall if m == 0 else [], inc=self.VE)
            r2_toks.append(t_r2)
            sq = self.TTB if m % 2 == 0 else self.SCB
            t_sq = E("scalar", lambda e, m=m, sq=sq: e.activation(
                sq[:], self.ARA[:, m * TW:(m + 1) * TW], AF.Square),
                waits=tokmax(t_r2, t_sq_prev, pn_toks[m - 2] if m >= 2 else None),
                inc=self.AC)
            t_sq_prev = t_sq
            first = m == 0
            last = m == KB_D - 1
            for n0 in (0, SUB):
                self.emit("tensor", lambda e, m=m, n0=n0, first=first, last=last: e.matmul(
                    s1.t[0:1, n0:n0 + SUB], self.CBF[:, 1160:1161],
                    self.ARA[:, m * TW + n0:m * TW + n0 + SUB], start=first, stop=last),
                    waits=tokmax(t_r2, (s1.done, s1.r) if first and n0 == 0 else None),
                    inc=s1.rdy if (last and n0 == SUB) else None)
                inc_s2 = (s2.rdy if last else self.PN) if n0 == SUB else None
                tk2 = self.emit("tensor", lambda e, n0=n0, sq=sq, first=first, last=last: e.matmul(
                    s2.t[0:1, n0:n0 + SUB], self.CBF[:, 1160:1161],
                    sq[:, n0:n0 + SUB], start=first, stop=last),
                    waits=tokmax(t_sq, (s2.done, s2.r) if first and n0 == 0 else None),
                    inc=inc_s2)
                if n0 == SUB:
                    pn_toks.append(tk2)
        out_dma = [None] * KB_D

        def dst_fn(m):
            return self.YS[:, (m % 2) * TW:(m % 2) * TW + TW]

        def dst_wait(m):
            return (self.OUTS[m % 2], 16 * (m // 2)) if m >= 2 else None

        x2_toks = self.ln_norm(lambda m: self.ARA[:, m * TW:(m + 1) * TW],
                               dst_fn, r2_toks, dst_wait_fn=dst_wait)
        for m in range(KB_D):
            E("sync", lambda e, m=m: e.dma_start(
                out=self.d_out[m * 128:(m + 1) * 128, :],
                in_=self.YS[:, (m % 2) * TW:(m % 2) * TW + TW]),
                waits=[x2_toks[m]], inc=self.OUTS[m % 2], inc_by=16)
        E("sync", lambda e: e.nop(),
          waits=[(self.OUTS[0], 16 * (KB_D // 2)), (self.OUTS[1], 16 * (KB_D // 2))])

    def emit_taps(self, tap_srcs):
        allw = [(self.VE, self.VE.n), (self.AC, self.AC.n), (self.GD, self.GD.n)]
        allw += [(pb.rdy, pb.w) for pb in self.ps]
        allw += [(pb.done, pb.r) for pb in self.ps]
        n = 0
        for name, src in tap_srcs.items():
            self.emit("gpsimd", lambda e, name=name, src=src: e.dma_start(
                out=self.tap_outs[name], in_=src), waits=allw, inc=self.OUT, inc_by=16)
            n += 1
        if n:
            self.emit("sync", lambda e: e.nop(),
                      waits=[(self.OUT, self.OUT.n)])

    def phases(self, upto="all"):
        self.prep()
        t_xs = self.load_xs(0, [])
        t_state_tok = self.t_memset[-1] if self.t_memset else None
        t_aux = list(self.t_memset)
        resid_all = []
        for sub in range(NSUB):
            u_toks, t_a1_grp = self.a1(sub, t_xs, t_aux)
            right_toks, t_chain = self.a34(sub, u_toks, t_state_tok)
            if sub == 1:
                t_state_tok = self.mask_state(t_chain)
            else:
                t_state_tok = t_chain
            if sub >= 2:
                lr_toks = self.a2(sub, right_toks, t_xs)
                r_toks, t_pe_stats = self.a5(sub, lr_toks, t_xs, t_chain)
                resid_all.extend(r_toks)
                t_xs_dep = tokmax(*r_toks)
                t_aux = [t_state_tok, t_pe_stats]
            else:
                t_xs_dep = tokmax(t_a1_grp)
                t_aux = [t_state_tok]
            if sub < NSUB - 1:
                t_xs = self.load_xs(sub + 1, t_xs_dep)
        if upto == "attn":
            return {"resid": resid_all}
        x1_toks = self.ln_norm(lambda m: self.RES[:, m * TW:(m + 1) * TW],
                               lambda m: self.RES[:, m * TW:(m + 1) * TW], resid_all)
        if upto == "ln1":
            return {"x1": x1_toks}
        self.gating(x1_toks)
        if upto == "gating":
            return {"x1": x1_toks}
        t_free = x1_toks
        t_mu_prev = None
        t_y = None
        for f in range(NF):
            t_xe_all, t_y_new = self.expert(f, t_free, t_mu_prev, t_y)
            t_mu_prev = t_xe_all
            t_free = t_y_new
            t_y = t_y_new
        if upto == "experts":
            return {"y": t_y}
        t_ysh = self.shared(t_y, t_y)
        if upto == "shared":
            return {"y": t_ysh}
        self.final(t_ysh)
        return {}

    def build_block(self):
        with self.nc.Block() as blk:
            for eng in self.ENGS:
                ops = self.ops[eng]
                if not ops:
                    continue
                getattr(blk, eng)(lambda e, ops=ops: [op(e) for op in ops])
        return self.nc


def build_kernel(taps=(), upto="all", tap_src_fn=None):
    k = K(taps)
    k.declare()
    cx = ExitStack()
    with cx:
        k.alloc(cx)
        k.phases(upto)
        assert len(k.ring_marks) >= k.ring_piece - 2, (len(k.ring_marks), k.ring_piece)
        if tap_src_fn:
            k.emit_taps(tap_src_fn(k))
        k.build_block()
    return k



_CACHE = {}


def _host_inputs(inputs):
    plan = wstream_plan(inputs)
    ws = np.concatenate([a.ravel() for (_, _, _, a) in plan]).astype(np.float32)
    vecs = host_vecs(inputs)
    cbf = host_cbf()
    cf32 = host_cf32()
    inv = np.ascontiguousarray(np.asarray(inputs["L0_ffn_inv_mapping"], np.int32).reshape(1, 64))
    gsc = np.ascontiguousarray(np.asarray(inputs["L0_ffn_gate_scale"], np.float32))
    usc = np.ascontiguousarray(np.asarray(inputs["L0_ffn_up_scale"], np.float32))
    dsc = np.ascontiguousarray(np.asarray(inputs["L0_ffn_down_scale"], np.float32))
    x = np.asarray(inputs["L0_attn_x"], np.float32)
    maps = []
    for c in range(8):
        b, h = c // 2, c % 2
        xt = np.ascontiguousarray(x[b].T)
        if h == 0:
            xt = np.ascontiguousarray(np.concatenate([xt[:, :TW], xt[:, :TW]], axis=1))
        hm = np.full((128, 1), float(h), np.float32)
        maps.append({"wstream": ws, "xt": xt, "vecs": vecs, "cbf": cbf, "cf32": cf32,
                     "invm": inv, "hmask": hm, "gsc": gsc, "usc": usc, "dsc": dsc})
    return maps


def kernel(**inputs):
    if "nc" not in _CACHE:
        _CACHE["nc"] = build_kernel().nc
    nc = _CACHE["nc"]
    maps = _host_inputs(inputs)
    res = run_bass_kernel_spmd(nc, maps, core_ids=list(range(8)))
    out = np.empty((4, 2048, D), np.float32)
    for c in range(8):
        b, h = c // 2, c % 2
        out[b, h * TW:(h + 1) * TW, :] = res.results[c]["out"].T
    return out


# version 08 (final)
# speedup vs baseline: 1.2804x; 1.2804x over previous
import numpy as np
import ml_dtypes
from contextlib import ExitStack

import concourse.bass as bass
import concourse.mybir as mybir
from concourse.bass_utils import run_bass_kernel_spmd
from concourse.bass import broadcast_tensor_aps

F32 = mybir.dt.float32
BF16 = mybir.dt.bfloat16
I32 = mybir.dt.int32
AF = mybir.ActivationFunctionType
ALU = mybir.AluOpType
AX = mybir.AxisListType
bfd = ml_dtypes.bfloat16

D = 2048
HID = 2560
II = 1408
ISH = 2816
NF = 8
TW = 1024
SUB = 512
NSUB = 4
KB_D, KB_H, KB_I, KB_S = 16, 20, 11, 22
RING_SLOT = 2816
RING_N = 4
LN_EPS = 1e-5


def _pack_lhsT_piece(W, m0, mc):
    kb = W.shape[0] // 128
    return np.ascontiguousarray(
        W[:, m0:m0 + mc].reshape(kb, 128, mc).transpose(1, 0, 2).reshape(128, kb * mc)
    ).astype(np.float32)


def wstream_plan(ins):
    plan = []

    def pieces(W, K, M, mp128):
        out = []
        step = mp128 * 128
        for m0 in range(0, M, step):
            mc = min(step, M - m0)
            out.append((K // 128, mc, None if W is None else _pack_lhsT_piece(W, m0, mc)))
        return out

    def addp(tag, plist):
        for (kb, mc, arr) in plist:
            plan.append((tag, kb, mc, arr))

    def raw(tag, arr, P, C):
        a = None if arr is None else np.ascontiguousarray(arr, dtype=np.float32).reshape(P, C)
        plan.append((tag, None, (P, C), a))

    none = ins is None
    g = (lambda k: None) if none else (lambda k: np.asarray(ins[k], np.float32))

    gw = None if none else np.ascontiguousarray(g("L0_ffn_gate_weight").T)
    raw("gwT", None if none else
        gw.reshape(KB_D, 128, 64).transpose(1, 0, 2).reshape(128, KB_D * 64), 128, KB_D * 64)

    Wr = None if none else g("L0_attn_Wr")
    Wa = None if none else g("L0_attn_Wa")
    Wi = None if none else g("L0_attn_Wi")
    Wl = None if none else g("L0_attn_Wl")
    Wo = None if none else g("L0_attn_Wo")
    for sub in range(NSUB):
        addp("Wr%d" % sub, pieces(Wr, D, HID, 1))
        pa = pieces(Wa, HID, HID, 1)
        pi = pieces(Wi, HID, HID, 1)
        for a, b in zip(pa, pi):
            plan.append(("Wa%d" % sub, a[0], a[1], a[2]))
            plan.append(("Wi%d" % sub, b[0], b[1], b[2]))
        if sub >= 2:
            addp("Wl%d" % sub, pieces(Wl, D, HID, 1))
            addp("Wo%d" % sub, pieces(Wo, HID, D, 1))

    for f in range(NF):
        raw("mu%d" % f, None if none else np.asarray(ins["L0_ffn_mask_up_W"], np.float32)[f].T, 8, D)
        qa = None if none else np.concatenate(
            [np.asarray(ins["L0_ffn_gate_qa"], np.float32)[f].reshape(64, D).T,
             np.asarray(ins["L0_ffn_up_qa"], np.float32)[f].reshape(64, D).T], axis=1)
        addp("qa%d" % f, pieces(qa, D, 128, 1))
        raw("qbg%d" % f, None if none else
            np.asarray(ins["L0_ffn_gate_qb"], np.float32)[f].transpose(0, 2, 1).reshape(64, II), 64, II)
        addp("gW%d" % f, pieces(None if none else np.asarray(ins["L0_ffn_gate_W"], np.float32)[f].T,
                                D, II, 1))
        raw("qbu%d" % f, None if none else
            np.asarray(ins["L0_ffn_up_qb"], np.float32)[f].transpose(0, 2, 1).reshape(64, II), 64, II)
        addp("uW%d" % f, pieces(None if none else np.asarray(ins["L0_ffn_up_W"], np.float32)[f].T,
                                D, II, 1))
        addp("dqa%d" % f, pieces(None if none else
                                 np.asarray(ins["L0_ffn_down_qa"], np.float32)[f].reshape(64, II).T,
                                 II, 64, 1))
        raw("qbd%d" % f, None if none else
            np.asarray(ins["L0_ffn_down_qb"], np.float32)[f].transpose(0, 2, 1).reshape(64, D), 64, D)
        addp("dW%d" % f, pieces(None if none else np.asarray(ins["L0_ffn_down_W"], np.float32)[f].T,
                                II, D, 1))

    addp("shg", pieces(None if none else g("L0_ffn_sh_gate_W").T, D, ISH, 1))
    addp("shu", pieces(None if none else g("L0_ffn_sh_up_W").T, D, ISH, 1))
    addp("shd", pieces(None if none else g("L0_ffn_sh_down_W").T, ISH, D, 1))
    return plan


def host_vecs(ins):
    v = np.zeros((128, 272), np.float32)

    def put(col, vec, nt):
        v[:, col:col + nt] = np.asarray(vec, np.float32).reshape(nt, 128).T

    put(0, ins["L0_attn_br"], KB_H)
    put(20, ins["L0_attn_bl"], KB_H)
    put(40, ins["L0_attn_ba"], KB_H)
    put(60, ins["L0_attn_bi"], KB_H)
    put(80, ins["L0_attn_conv_b"], KB_H)
    put(100, ins["L0_attn_lam"], KB_H)
    put(120, ins["L0_attn_bo"], KB_D)
    cw = np.asarray(ins["L0_attn_conv_w"], np.float32).reshape(4, HID)
    for ci in range(KB_H):
        for j in range(4):
            v[:, 136 + ci * 4 + j] = cw[j, ci * 128:(ci + 1) * 128]
    return v


def host_cbf():
    c = np.zeros((128, 1292), np.float32)
    c[:, 0:128] = np.eye(128)
    for f in range(NF):
        for j in range(128):
            c[f * 8 + ((j % 64) // 8), 128 + f * 128 + j] = 1.0
    for fk in range(64):
        c[fk, 1152 + fk // 8] = 1.0
    c[:, 1160] = 1.0
    c[0, 1160:1288] = 1.0
    return c.astype(bfd)


def host_cf32():
    c = np.zeros((128, 262), np.float32)
    c[:, 0:128] = np.eye(128)
    c[0:64, 128] = np.arange(64)
    c[0, 130:258] = 1.0
    return c



class Sem:
    __slots__ = ("h", "n")

    def __init__(self, h):
        self.h = h
        self.n = 0


class PB:
    __slots__ = ("t", "rdy", "done", "w", "r")

    def __init__(self, t, rdy, done):
        self.t = t
        self.rdy = rdy
        self.done = done
        self.w = 0
        self.r = 0


def tokmax(*toks):
    best = {}
    stack = list(toks)
    while stack:
        t = stack.pop()
        if t is None:
            continue
        if isinstance(t, list):
            stack.extend(t)
            continue
        s, v = t
        cur = best.get(id(s))
        if cur is None or cur[1] < v:
            best[id(s)] = (s, v)
    return list(best.values())


class K:
    ENGS = ("sync", "scalar", "vector", "tensor", "gpsimd")

    YS_RS = (0, 516)
    YS_UACC = 1032
    YS_R = 1544
    YS_I = 2056
    YS_A = 2568
    YS_E2 = 3080
    YS_DV = 3592
    YS_SC = 4104
    YS_T5 = 1544
    RW_MEAN = 4616
    RW_RSTD = 5640
    RW_MS = 6664
    RW_VAR = 7688

    def __init__(self, taps=()):
        self.taps = list(taps)
        self.nc = bass.Bass("TRN2", target_bir_lowering=False)
        self.ops = {e: [] for e in self.ENGS}
        self.lastw = {e: {} for e in self.ENGS}
        self.plan = wstream_plan(None)
        self.plan_i = 0
        self.ws_cursor = 0
        self.ring_piece = 0
        self.ring_marks = []
        self.tap_outs = {}

    def emit(self, eng, fn, waits=(), inc=None, inc_by=1):
        lw = self.lastw[eng]
        real = []
        for s, v in (tokmax(*waits) if waits else []):
            if v > 0 and lw.get(id(s), 0) < v:
                lw[id(s)] = v
                real.append((s, v))
        tok = None
        if inc is not None:
            inc.n += inc_by
            tok = (inc, inc.n)

        def op(e, fn=fn, real=tuple(real), inc=inc, inc_by=inc_by):
            for s, v in real:
                e.wait_ge(s.h, v)
            inst = fn(e)
            if inc is not None:
                inst.then_inc(inc.h, inc_by)
        self.ops[eng].append(op)
        return tok

    def mm(self, pb, fns, waits=(), begin=True, end=True):
        if begin:
            pb.w += 1
            w = tokmax((pb.done, pb.r), *waits)
        else:
            w = tokmax(*waits)
        tok = None
        for i, fn in enumerate(fns):
            last = end and (i == len(fns) - 1)
            t = self.emit("tensor", fn, waits=w if i == 0 else (),
                          inc=pb.rdy if last else None)
            if last:
                tok = t
        return tok

    def rd(self, pb, eng, fn, waits=(), final=True, inc=None):
        w = tokmax((pb.rdy, pb.w), *waits)
        if final:
            pb.r += 1
            return self.emit(eng, fn, waits=w, inc=pb.done)
        return self.emit(eng, fn, waits=w, inc=inc)

    def declare(self):
        nc = self.nc
        total = sum((kb * 128 * mc if kb is not None else mc[0] * mc[1])
                    for (_, kb, mc, _) in self.plan)
        self.ws_total = total
        self.d_ws = nc.dram_tensor("wstream", [total], F32, kind="ExternalInput").ap()
        self.d_xt = nc.dram_tensor("xt", [D, 2048], F32, kind="ExternalInput").ap()
        self.d_vec = nc.dram_tensor("vecs", [128, 272], F32, kind="ExternalInput").ap()
        self.d_cbf = nc.dram_tensor("cbf", [128, 1292], BF16, kind="ExternalInput").ap()
        self.d_cf32 = nc.dram_tensor("cf32", [128, 262], F32, kind="ExternalInput").ap()
        self.d_inv = nc.dram_tensor("invm", [1, 64], I32, kind="ExternalInput").ap()
        self.d_hmask = nc.dram_tensor("hmask", [128, 1], F32, kind="ExternalInput").ap()
        self.d_gsc = nc.dram_tensor("gsc", [NF, II], F32, kind="ExternalInput").ap()
        self.d_usc = nc.dram_tensor("usc", [NF, II], F32, kind="ExternalInput").ap()
        self.d_dsc = nc.dram_tensor("dsc", [NF, D], F32, kind="ExternalInput").ap()
        self.d_out = nc.dram_tensor("out", [D, TW], F32, kind="ExternalOutput").ap()
        for name, shape in self.taps:
            self.tap_outs[name] = nc.dram_tensor(
                "tap_" + name, list(shape), F32, kind="ExternalOutput").ap()

    def alloc(self, cx):
        nc = self.nc
        sb = lambda n, s, d: cx.enter_context(nc.sbuf_tensor(n, s, d))
        self.RING = sb("RING", [128, RING_N * RING_SLOT], BF16)
        self.ARA = sb("ARA", [128, 28672], BF16)
        self.RES = sb("RES", [128, 16384], BF16)
        self.CBF = sb("CBF", [128, 1292], BF16)
        self.MUT = sb("MUT", [64, 2048], BF16)
        self.SOFTC = sb("SOFTC", [64, 1024], BF16)
        self.WDC = sb("WDC", [64, 1024], BF16)
        self.GWT = sb("GWT", [128, KB_D * 64], BF16)
        self.PBF = sb("PBF", [64, 64], BF16)
        self.PTBF = sb("PTBF", [64, 64], BF16)
        self.PGBF = sb("PGBF", [64, 8], BF16)
        self.QROW = sb("QROW", [1, 2048], BF16)
        self.QB = sb("QB", [64, 2048], BF16)
        self.SCB = sb("SCB", [128, 1024], BF16)
        self.AW = sb("AW", [64, 2048], BF16)
        self.TTB = sb("TTB", [128, 1024], BF16)
        self.YS = sb("YS", [128, 16384], F32)
        self.VEC = sb("VEC", [128, 272], F32)
        self.HST = sb("HST", [128, 128], F32)
        self.GT = sb("GT", [128, 512], F32)
        self.CF32 = sb("CF32", [128, 262], F32)
        self.INV = sb("INV", [1, 64], I32)

        mkps = lambda n, w: cx.enter_context(nc.psum_tensor(n, [128, w], F32))
        mks = lambda n: Sem(cx.enter_context(nc.semaphore(n)))
        self.psa = [PB(mkps("PA%d" % i, 512), mks("par%d" % i), mks("pad%d" % i))
                    for i in range(4)]
        self.ps = [PB(mkps("PF%d" % i, 1024), mks("pfr%d" % i), mks("pfd%d" % i))
                   for i in range(2)]
        self.ps = [self.ps[0], self.ps[1], self.ps[0], self.ps[1]]
        self.LD = mks("LD")
        self.GD = mks("GD")
        self.RG = [mks("RG%d" % i) for i in range(RING_N)]
        self.XS = mks("XS")
        self.MU = mks("MU")
        self.QBS = mks("QBS")
        self.GW = mks("GW")
        self.OUTS = [mks("OUT0"), mks("OUT1")]
        self.RF = mks("RF")
        self.VE = mks("VE")
        self.AC = mks("AC")
        self.PN = mks("PN")
        self.OUT = mks("OUT")

    def ws_next(self, tag_expect, dest=None, dest_part=0, waits=(), sem=None):
        tag, kb, mc, _ = self.plan[self.plan_i]
        assert tag == tag_expect, (tag, tag_expect, self.plan_i)
        self.plan_i += 1
        if kb is not None:
            nelem = kb * 128 * mc
            src = self.d_ws[self.ws_cursor:self.ws_cursor + nelem].rearrange("(p c) -> p c", p=128)
            piece = self.ring_piece
            base = (piece % RING_N) * RING_SLOT
            dst = self.RING[:, base:base + kb * mc]
            w = list(waits)
            if piece >= RING_N:
                assert len(self.ring_marks) >= piece - RING_N + 1, (piece, len(self.ring_marks))
                w.append(self.ring_marks[piece - RING_N])
            tok = self.emit("gpsimd", lambda e, dst=dst, src=src: e.dma_start(out=dst, in_=src),
                            waits=w, inc=self.RG[piece % RING_N], inc_by=16)
            self.ring_piece += 1
            self.ws_cursor += nelem
            return base, kb, mc, tok
        else:
            P_, C_ = mc
            nelem = P_ * C_
            src = self.d_ws[self.ws_cursor:self.ws_cursor + nelem].rearrange("(p c) -> p c", p=P_)
            dst = dest[dest_part:dest_part + P_, 0:C_]
            tok = self.emit("gpsimd", lambda e, dst=dst, src=src: e.dma_start(out=dst, in_=src),
                            waits=waits, inc=sem if sem is not None else self.GD, inc_by=16)
            self.ws_cursor += nelem
            return None, P_, C_, tok

    def ring_free(self, tok):
        self.ring_marks.append(tok)

    def c_xs(self, kb):
        return kb * SUB

    def c_u(self, ci):
        return 8192 + ci * SUB

    def c_rt(self, ci):
        return 18432 + ci * SUB

    def c_xe(self, m):
        return m * TW

    def c_gate(self, m):
        return 16384 + m * TW

    def c_sh(self, ms):
        return 16384 + ms * TW if ms < KB_I else (ms - KB_I) * TW

    def prep(self):
        E = self.emit
        t_vec = E("sync", lambda e: e.dma_start(out=self.VEC[:], in_=self.d_vec),
                  inc=self.LD, inc_by=16)
        t_cbf = E("sync", lambda e: e.dma_start(out=self.CBF[:], in_=self.d_cbf),
                  inc=self.LD, inc_by=16)
        t_cf = E("sync", lambda e: e.dma_start(out=self.CF32[:], in_=self.d_cf32),
                 inc=self.LD, inc_by=16)
        t_inv = E("sync", lambda e: e.dma_start(out=self.INV[:], in_=self.d_inv),
                  inc=self.LD, inc_by=16)
        t_hm = E("sync", lambda e: e.dma_start(out=self.HST[:, 104:105], in_=self.d_hmask),
                 inc=self.LD, inc_by=16)
        self.t_consts = tokmax(t_vec, t_cbf, t_cf, t_inv, t_hm)

        _, _, _, self.t_gwt = self.ws_next("gwT", dest=self.GWT, sem=self.GW)

        t_m1 = E("vector", lambda e: e.memset(self.HST[:, 0:104], 0.0), inc=self.VE)
        t_m3 = E("vector", lambda e: e.memset(self.HST[:, 100:101], LN_EPS),
                 waits=[t_m1], inc=self.VE)
        self.t_memset = tokmax(t_m1, t_m3)

        t1 = E("scalar", lambda e: e.activation(
            self.YS[:, 0:20], self.VEC[:, 100:120], AF.Exp, scale=-1.0),
            waits=[*self.t_consts], inc=self.AC)
        t2 = E("scalar", lambda e: e.activation(
            self.YS[:, 20:40], self.YS[:, 0:20], AF.Ln, bias=1.0, scale=1.0),
            waits=[t1], inc=self.AC)
        self.t_n8 = E("vector", lambda e: e.tensor_scalar(
            self.VEC[:, 216:236], self.YS[:, 20:40], -8.0, None, ALU.mult),
            waits=[t2], inc=self.VE)

        t_if = E("vector", lambda e: e.tensor_copy(self.GT[0:1, 64:128], self.INV[:]),
                 waits=[*self.t_consts], inc=self.VE)
        pb = self.ps[2]
        self.mm(pb, [lambda e: e.matmul(pb.t[0:64, 0:64], self.CF32[0:1, 130:194],
                                        self.GT[0:1, 64:128], start=True, stop=True)],
                waits=[t_if, *self.t_memset])
        t_p = self.rd(pb, "vector", lambda e: e.tensor_scalar(
            self.PBF[:], pb.t[0:64, 0:64], self.CF32[0:64, 128:129], None, ALU.is_equal),
            final=False, inc=self.VE)
        t_p32 = self.rd(pb, "vector", lambda e: e.tensor_scalar(
            self.GT[0:64, 0:64], pb.t[0:64, 0:64], self.CF32[0:64, 128:129], None, ALU.is_equal),
            waits=[t_p], final=True)
        pb3 = self.ps[3]
        self.mm(pb3, [lambda e: e.transpose(pb3.t[0:64, 0:64], self.GT[0:64, 0:64],
                                            self.CF32[0:64, 0:64])], waits=[t_p32])
        t_pt = self.rd(pb3, "vector", lambda e: e.tensor_copy(self.PTBF[:], pb3.t[0:64, 0:64]))
        self.mm(pb3, [lambda e: e.matmul(pb3.t[0:64, 0:8], self.PTBF[:],
                                         self.CBF[0:64, 1152:1160], start=True, stop=True)],
                waits=[t_pt, *self.t_consts])
        self.t_pg = self.rd(pb3, "vector", lambda e: e.tensor_copy(self.PGBF[:], pb3.t[0:64, 0:8]))
        self.t_pbf = t_p

    def load_xs(self, sub, wait_toks):
        c0 = sub * SUB
        src = self.d_xt.rearrange("(kb p) t -> p kb t", p=128)[:, :, c0:c0 + SUB]
        dst = self.ARA[:, 0:8192].rearrange("p (kb t) -> p kb t", t=SUB)
        return self.emit("gpsimd", lambda e: e.dma_start(out=dst, in_=src),
                         waits=wait_toks, inc=self.XS, inc_by=16)

    def a1(self, sub, t_xs, t_aux):
        u_toks = [None] * KB_H
        chain_hist = [None, None]
        t_grp_last = None
        ci = 0
        for p in range((HID + 127) // 128):
            base, kb, mc, t_ld = self.ws_next("Wr%d" % sub)
            t_grp = None
            for mi in range(mc // 128):
                pb = self.psa[ci % 4]
                fns = []
                for k in range(KB_D):
                    def f(e, k=k, pb=pb, base=base, mc=mc, mi=mi):
                        o = base + k * mc + mi * 128
                        return e.matmul(pb.t[:, 0:SUB], self.RING[:, o:o + 128],
                                        self.ARA[:, self.c_xs(k):self.c_xs(k) + SUB],
                                        start=(k == 0), stop=(k == KB_D - 1))
                    fns.append(f)
                t_grp = self.mm(pb, fns, waits=[t_ld, t_xs])
                t_grp_last = t_grp
                off = (ci % 2) * 8712
                rs = off + 0
                t_ev = self.rd(pb, "scalar", lambda e, rs=rs, ci=ci, pb=pb: e.activation(
                    self.YS[:, rs + 3: rs + 3 + SUB], pb.t[:, 0:SUB], AF.Identity,
                    bias=self.VEC[:, ci:ci + 1], scale=1.0),
                    waits=tokmax(self.t_n8, *(t_aux if ci == 0 else [])), final=True)
                ch = self.conv_chain(ci, rs, off, t_ev, chain_hist[ci % 2],
                                     t_aux if ci <= 1 else [])
                u_toks[ci] = ch
                chain_hist[ci % 2] = ch
                ci += 1
            self.ring_free(t_grp)
        assert ci == KB_H
        return u_toks, t_grp_last

    def conv_chain(self, ci, rs, off, t_ev, t_prev, t_aux):
        E = self.emit
        UA = self.YS_UACC + off
        t_h = E("vector", lambda e, rs=rs, ci=ci: e.tensor_copy(
            self.YS[:, rs:rs + 3], self.HST[:, ci * 3:ci * 3 + 3]),
            waits=tokmax(t_ev, t_prev, *t_aux), inc=self.VE)
        t0 = E("scalar", lambda e, rs=rs, ci=ci, UA=UA: e.activation(
            self.YS[:, UA:UA + SUB], self.YS[:, rs:rs + SUB],
            AF.Identity, bias=self.VEC[:, 80 + ci:81 + ci],
            scale=self.VEC[:, 136 + ci * 4:137 + ci * 4]),
            waits=[t_h], inc=self.AC)
        t1 = E("vector", lambda e, rs=rs, ci=ci, UA=UA: e.scalar_tensor_tensor(
            self.YS[:, UA:UA + SUB], self.YS[:, rs + 1:rs + 1 + SUB],
            self.VEC[:, 137 + ci * 4:138 + ci * 4], self.YS[:, UA:UA + SUB],
            ALU.mult, ALU.add), waits=[t0], inc=self.VE)
        t2 = E("vector", lambda e, rs=rs, ci=ci, UA=UA: e.scalar_tensor_tensor(
            self.YS[:, UA:UA + SUB], self.YS[:, rs + 2:rs + 2 + SUB],
            self.VEC[:, 138 + ci * 4:139 + ci * 4], self.YS[:, UA:UA + SUB],
            ALU.mult, ALU.add), waits=[t1], inc=self.VE)
        t_sv = E("vector", lambda e, rs=rs, ci=ci: e.tensor_copy(
            self.HST[:, ci * 3:ci * 3 + 3], self.YS[:, rs + SUB:rs + SUB + 3]),
            waits=[t2], inc=self.VE)
        t3 = E("vector", lambda e, rs=rs, ci=ci, UA=UA: e.scalar_tensor_tensor(
            self.ARA[:, self.c_u(ci):self.c_u(ci) + SUB], self.YS[:, rs + 3:rs + 3 + SUB],
            self.VEC[:, 139 + ci * 4:140 + ci * 4], self.YS[:, UA:UA + SUB],
            ALU.mult, ALU.add), waits=[t_sv], inc=self.VE)
        return t3

    def a34(self, sub, u_toks, t_state0):
        E = self.emit
        t_u_all = tokmax(*u_toks)
        chain_hist = [t_state0, t_state0]
        t_chain = t_state0
        right_toks = [None] * KB_H
        for p in range((HID + 127) // 128):
            base_a, kba, mca, t_lda = self.ws_next("Wa%d" % sub)
            base_i, kbi, mci, t_ldi = self.ws_next("Wi%d" % sub)
            t_grp_a = t_grp_i = None
            for mi in range(mca // 128):
                ci = p
                pa = self.psa[(2 * ci) % 4]
                pi_ = self.psa[(2 * ci + 1) % 4]
                off = (ci % 2) * 8712
                R_, I_, A_, E2, DV, SC = (self.YS_R + off, self.YS_I + off,
                                          self.YS_A + off, self.YS_E2 + off,
                                          self.YS_DV + off, self.YS_SC + off)
                fa = []
                fi = []
                for k in range(KB_H):
                    def f1(e, k=k, base=base_a, mc=mca, mi=mi, pb=pa):
                        o = base + k * mc + mi * 128
                        return e.matmul(pb.t[:, 0:SUB], self.RING[:, o:o + 128],
                                        self.ARA[:, self.c_u(k):self.c_u(k) + SUB],
                                        start=(k == 0), stop=(k == KB_H - 1))

                    def f2(e, k=k, base=base_i, mc=mci, mi=mi, pb=pi_):
                        o = base + k * mc + mi * 128
                        return e.matmul(pb.t[:, 0:SUB], self.RING[:, o:o + 128],
                                        self.ARA[:, self.c_u(k):self.c_u(k) + SUB],
                                        start=(k == 0), stop=(k == KB_H - 1))
                    fa.append(f1)
                    fi.append(f2)
                t_grp_a = self.mm(pa, fa, waits=[t_lda, t_u_all])
                t_grp_i = self.mm(pi_, fi, waits=[t_ldi, t_u_all])
                t_r = self.rd(pa, "scalar", lambda e, ci=ci, pb=pa, R_=R_: e.activation(
                    self.YS[:, R_:R_ + SUB], pb.t[:, 0:SUB], AF.Sigmoid,
                    bias=self.VEC[:, 40 + ci:41 + ci], scale=1.0),
                    waits=tokmax(chain_hist[ci % 2]), final=True)
                t_i = self.rd(pi_, "scalar", lambda e, ci=ci, pb=pi_, I_=I_: e.activation(
                    self.YS[:, I_:I_ + SUB], pb.t[:, 0:SUB], AF.Sigmoid,
                    bias=self.VEC[:, 60 + ci:61 + ci], scale=1.0),
                    waits=[t_r], final=True)
                tla = E("vector", lambda e, ci=ci, R_=R_: e.tensor_scalar(
                    self.YS[:, R_:R_ + SUB], self.YS[:, R_:R_ + SUB],
                    self.VEC[:, 216 + ci:217 + ci], None, ALU.mult),
                    waits=[t_r], inc=self.VE)
                ta = E("scalar", lambda e, R_=R_, A_=A_: e.activation(
                    self.YS[:, A_:A_ + SUB], self.YS[:, R_:R_ + SUB],
                    AF.Exp), waits=[tla], inc=self.AC)
                ta2 = E("vector", lambda e, A_=A_, E2=E2: e.tensor_mul(
                    self.YS[:, E2:E2 + SUB], self.YS[:, A_:A_ + SUB],
                    self.YS[:, A_:A_ + SUB]), waits=[ta], inc=self.VE)
                tb2 = E("vector", lambda e, E2=E2: e.tensor_scalar(
                    self.YS[:, E2:E2 + SUB], self.YS[:, E2:E2 + SUB],
                    -1.0, 1.0, ALU.mult, ALU.add), waits=[ta2], inc=self.VE)
                tb2b = E("vector", lambda e, E2=E2: e.tensor_scalar(
                    self.YS[:, E2:E2 + SUB], self.YS[:, E2:E2 + SUB],
                    1e-12, None, ALU.max), waits=[tb2], inc=self.VE)
                tbl = E("scalar", lambda e, E2=E2: e.activation(
                    self.YS[:, E2:E2 + SUB], self.YS[:, E2:E2 + SUB],
                    AF.Ln), waits=[tb2b], inc=self.AC)
                tbe = E("scalar", lambda e, E2=E2: e.activation(
                    self.YS[:, E2:E2 + SUB], self.YS[:, E2:E2 + SUB],
                    AF.Exp, scale=0.5), waits=[tbl], inc=self.AC)
                tdv = E("vector", lambda e, ci=ci, I_=I_, DV=DV: e.tensor_mul(
                    self.YS[:, DV:DV + SUB], self.YS[:, I_:I_ + SUB],
                    self.ARA[:, self.c_u(ci):self.c_u(ci) + SUB]), waits=[t_i], inc=self.VE)
                tdv2 = E("vector", lambda e, DV=DV, E2=E2: e.tensor_mul(
                    self.YS[:, DV:DV + SUB], self.YS[:, DV:DV + SUB],
                    self.YS[:, E2:E2 + SUB]), waits=[tbe, tdv], inc=self.VE)
                if sub >= 2:
                    tsc = E("vector", lambda e, ci=ci, A_=A_, DV=DV: e.tensor_tensor_scan(
                        self.ARA[:, self.c_rt(ci):self.c_rt(ci) + SUB],
                        self.YS[:, A_:A_ + SUB],
                        self.YS[:, DV:DV + SUB],
                        self.HST[:, 60 + ci:61 + ci], ALU.mult, ALU.add),
                        waits=[ta, tdv2], inc=self.VE)
                    tst = E("vector", lambda e, ci=ci: e.tensor_copy(
                        self.HST[:, 60 + ci:61 + ci],
                        self.ARA[:, self.c_rt(ci) + SUB - 1:self.c_rt(ci) + SUB]),
                        waits=[tsc], inc=self.VE)
                    right_toks[ci] = tsc
                else:
                    tsc = E("vector", lambda e, ci=ci, A_=A_, DV=DV, SC=SC: e.tensor_tensor_scan(
                        self.YS[:, SC:SC + SUB],
                        self.YS[:, A_:A_ + SUB],
                        self.YS[:, DV:DV + SUB],
                        self.HST[:, 60 + ci:61 + ci], ALU.mult, ALU.add),
                        waits=[ta, tdv2], inc=self.VE)
                    tst = E("vector", lambda e, ci=ci, SC=SC: e.tensor_copy(
                        self.HST[:, 60 + ci:61 + ci],
                        self.YS[:, SC + SUB - 1:SC + SUB]),
                        waits=[tsc], inc=self.VE)
                chain_hist[ci % 2] = tst
                t_chain = tst
            self.ring_free(t_grp_a)
            self.ring_free(t_grp_i)
        return right_toks, t_chain

    def mask_state(self, t_state):
        t1 = self.emit("vector", lambda e: e.tensor_scalar(
            self.HST[:, 60:80], self.HST[:, 60:80], self.HST[:, 104:105], None, ALU.mult),
            waits=tokmax(t_state, *self.t_consts), inc=self.VE)
        t2 = self.emit("vector", lambda e: e.tensor_scalar(
            self.HST[:, 0:60], self.HST[:, 0:60], self.HST[:, 104:105], None, ALU.mult),
            waits=[t1], inc=self.VE)
        return t2

    def a2(self, sub, right_toks, t_xs):
        E = self.emit
        lr_toks = [None] * KB_H
        prev_hist = [None, None]
        ci = 0
        for p in range((HID + 127) // 128):
            base, kb, mc, t_ld = self.ws_next("Wl%d" % sub)
            t_grp = None
            for mi in range(mc // 128):
                pb = self.psa[ci % 4]
                fns = []
                for k in range(KB_D):
                    def f(e, k=k, pb=pb, base=base, mc=mc, mi=mi):
                        o = base + k * mc + mi * 128
                        return e.matmul(pb.t[:, 0:SUB], self.RING[:, o:o + 128],
                                        self.ARA[:, self.c_xs(k):self.c_xs(k) + SUB],
                                        start=(k == 0), stop=(k == KB_D - 1))
                    fns.append(f)
                t_grp = self.mm(pb, fns, waits=[t_ld, t_xs])
                sl = (ci % 2) * SUB
                t_g = self.rd(pb, "scalar", lambda e, pb=pb, ci=ci, sl=sl: e.activation(
                    self.TTB[:, sl:sl + SUB], pb.t[:, 0:SUB], AF.Gelu_apprx_tanh,
                    bias=self.VEC[:, 20 + ci:21 + ci], scale=1.0),
                    waits=tokmax(prev_hist[ci % 2]), final=True)
                tlr = E("vector", lambda e, ci=ci, sl=sl: e.tensor_mul(
                    self.ARA[:, self.c_rt(ci):self.c_rt(ci) + SUB],
                    self.TTB[:, sl:sl + SUB],
                    self.ARA[:, self.c_rt(ci):self.c_rt(ci) + SUB]),
                    waits=tokmax(t_g, right_toks[ci]), inc=self.VE)
                lr_toks[ci] = tlr
                prev_hist[ci % 2] = tlr
                ci += 1
            self.ring_free(t_grp)
        return lr_toks

    def a5(self, sub, lr_toks, t_xs, t_slot_free):
        E = self.emit
        t_lr = tokmax(*lr_toks)
        wcol = (sub - 2) * SUB
        resid_toks = [None] * KB_D
        prev_hist = [t_slot_free, t_slot_free]
        m = 0
        s1, s2 = self.ps[2], self.ps[3]
        for p in range((D + 127) // 128):
            base, kb, mc, t_ld = self.ws_next("Wo%d" % sub)
            t_grp = None
            for mi in range(mc // 128):
                pb = self.psa[m % 4]
                T5 = self.YS_T5 + (m % 2) * 8712
                fns = []
                for k in range(KB_H):
                    def f(e, k=k, pb=pb, base=base, mc=mc, mi=mi):
                        o = base + k * mc + mi * 128
                        return e.matmul(pb.t[:, 0:SUB], self.RING[:, o:o + 128],
                                        self.ARA[:, self.c_rt(k):self.c_rt(k) + SUB],
                                        start=(k == 0), stop=(k == KB_H - 1))
                    fns.append(f)
                t_grp = self.mm(pb, fns, waits=[t_ld, *t_lr])
                t_e = self.rd(pb, "scalar", lambda e, pb=pb, m=m, T5=T5: e.activation(
                    self.YS[:, T5:T5 + SUB], pb.t[:, 0:SUB], AF.Identity,
                    bias=self.VEC[:, 120 + m:121 + m], scale=1.0),
                    waits=tokmax(prev_hist[m % 2]), final=True)
                rc = m * TW + wcol
                t_r = E("vector", lambda e, m=m, rc=rc, T5=T5: e.tensor_add(
                    self.RES[:, rc:rc + SUB], self.YS[:, T5:T5 + SUB],
                    self.ARA[:, self.c_xs(m):self.c_xs(m) + SUB]),
                    waits=tokmax(t_e, t_xs), inc=self.VE)
                resid_toks[m] = t_r
                sqc = 8192 + m * SUB
                t_sq = E("scalar", lambda e, rc=rc, sqc=sqc: e.activation(
                    self.ARA[:, sqc:sqc + SUB], self.RES[:, rc:rc + SUB], AF.Square),
                    waits=[t_r], inc=self.AC)
                prev_hist[m % 2] = t_sq
                first = (m == 0)
                if sub == 2 and m == 0:
                    s1.w += 1
                    s2.w += 1
                last = (sub == 3 and m == KB_D - 1)
                lastm = (m == KB_D - 1)
                self.emit("tensor", lambda e, rc=rc, wcol=wcol, first=first, lastm=lastm: e.matmul(
                    s1.t[0:1, wcol:wcol + SUB], self.CBF[:, 1160:1161],
                    self.RES[:, rc:rc + SUB], start=first, stop=lastm),
                    waits=tokmax(t_r, (s1.done, s1.r) if first else None),
                    inc=s1.rdy if last else None)
                inc2 = s2.rdy if last else (
                    self.PN if (sub == 2 and m == KB_D - 1) else None)
                self.emit("tensor", lambda e, sqc=sqc, wcol=wcol, first=first, lastm=lastm: e.matmul(
                    s2.t[0:1, wcol:wcol + SUB], self.CBF[:, 1160:1161],
                    self.ARA[:, sqc:sqc + SUB], start=first, stop=lastm),
                    waits=tokmax(t_sq, (s2.done, s2.r) if first else None),
                    inc=inc2)
                m += 1
            self.ring_free(t_grp)
        t_pe = (self.PN, self.PN.n) if sub == 2 else None
        return resid_toks, t_pe

    def ln_norm(self, src_fn, dst_fn, resid_all, dst_wait_fn=None):
        E = self.emit
        R = self.YS
        MEAN, RSTD, MS, VARC = self.RW_MEAN, self.RW_RSTD, self.RW_MS, self.RW_VAR
        s1, s2 = self.ps[2], self.ps[3]
        t_mean = self.rd(s1, "vector", lambda e: e.tensor_scalar(
            R[0:1, MEAN:MEAN + TW], s1.t[0:1, 0:TW], 1.0 / D, None, ALU.mult), final=True)
        t_ms = E("scalar", lambda e: e.activation(R[0:1, MS:MS + TW], R[0:1, MEAN:MEAN + TW],
                                                  AF.Square), waits=[t_mean], inc=self.AC)
        t_v1 = self.rd(s2, "vector", lambda e: e.tensor_scalar(
            R[0:1, VARC:VARC + TW], s2.t[0:1, 0:TW], 1.0 / D, None, ALU.mult), final=True)
        t_v2 = E("vector", lambda e: e.tensor_sub(
            R[0:1, VARC:VARC + TW], R[0:1, VARC:VARC + TW], R[0:1, MS:MS + TW]),
            waits=[t_ms, t_v1], inc=self.VE)
        t_sd = E("scalar", lambda e: e.activation(
            R[0:1, VARC:VARC + TW], R[0:1, VARC:VARC + TW], AF.Ln,
            bias=self.HST[0:1, 100:101], scale=1.0), waits=[t_v2], inc=self.AC)
        t_rst = E("scalar", lambda e: e.activation(
            R[0:1, RSTD:RSTD + TW], R[0:1, VARC:VARC + TW], AF.Exp,
            scale=-0.5), waits=[t_sd], inc=self.AC)
        pm, pr = self.ps[0], self.ps[1]
        self.mm(pm, [
            (lambda e, n0=n0: e.matmul(pm.t[:, n0:n0 + SUB], self.CF32[0:1, 130:258],
                                       R[0:1, MEAN + n0:MEAN + n0 + SUB], start=True, stop=True))
            for n0 in (0, SUB)], waits=[t_mean])
        self.mm(pr, [
            (lambda e, n0=n0: e.matmul(pr.t[:, n0:n0 + SUB], self.CF32[0:1, 130:258],
                                       R[0:1, RSTD + n0:RSTD + n0 + SUB], start=True, stop=True))
            for n0 in (0, SUB)], waits=[t_rst])
        out = []
        hist = [None, None]
        for m in range(KB_D):
            sl = 2048 + (m % 2) * 1024
            w1 = tokmax(*(resid_all if m == 0 else []), hist[m % 2])
            t_s = self.rd(pm, "vector", lambda e, m=m, sl=sl: e.tensor_sub(
                self.YS[:, sl:sl + TW], src_fn(m), pm.t[:, 0:TW]),
                waits=w1, final=True)
            w2 = [t_s] + ([dst_wait_fn(m)] if dst_wait_fn else [])
            t_n = self.rd(pr, "vector", lambda e, m=m, sl=sl: e.tensor_mul(
                dst_fn(m), self.YS[:, sl:sl + TW], pr.t[:, 0:TW]),
                waits=w2, final=True)
            hist[m % 2] = t_n
            out.append(t_n)
        return out

    def gating(self, x1_toks):
        E = self.emit
        G = self.GT
        t_x1 = tokmax(*x1_toks)
        t_wdc = [None] * 8
        t_sfc = [None] * 8
        prev = [None, None]
        for tt in range(8):
            o = (tt % 2) * 256
            pb_s = self.psa[tt % 2]
            pb_f = self.psa[2 + tt % 2]
            fns = []
            for k in range(KB_D):
                def f(e, k=k, tt=tt, pb_s=pb_s):
                    return e.matmul(pb_s.t[:, 0:64],
                                    self.RES[:, k * TW + tt * 128:k * TW + tt * 128 + 128],
                                    self.GWT[:, k * 64:(k + 1) * 64],
                                    start=(k == 0), stop=(k == KB_D - 1))
                fns.append(f)
            self.mm(pb_s, fns, waits=[*t_x1, self.t_gwt])
            t_nm = self.rd(pb_s, "vector", lambda e, o=o, pb_s=pb_s: e.reduce_max(
                G[:, o + 192:o + 193], pb_s.t[:, 0:64], axis=AX.X, negate=True),
                waits=tokmax(prev[tt % 2]), final=False, inc=self.VE)
            t_e = self.rd(pb_s, "scalar", lambda e, o=o, pb_s=pb_s: e.activation(
                G[:, o:o + 64], pb_s.t[:, 0:64], AF.Exp, bias=G[:, o + 192:o + 193],
                scale=1.0, accum_out=G[:, o + 193:o + 194]), waits=[t_nm], final=True)
            t_rc = E("vector", lambda e, o=o: e.reciprocal(G[:, o + 194:o + 195],
                                                           G[:, o + 193:o + 194]),
                     waits=[t_e], inc=self.VE)
            t_sc = E("vector", lambda e, o=o: e.tensor_scalar(
                G[:, o:o + 64], G[:, o:o + 64], G[:, o + 194:o + 195], None, ALU.mult),
                waits=[t_rc], inc=self.VE)
            t_w = E("vector", lambda e, o=o: e.tensor_copy(G[:, o + 64:o + 128], G[:, o:o + 64]),
                    waits=[t_sc], inc=self.VE)
            for r in range(5):
                t_m = E("vector", lambda e, o=o: e.reduce_max(
                    G[:, o + 195:o + 196], G[:, o + 64:o + 128], axis=AX.X),
                    waits=[t_w], inc=self.VE)
                t_k = E("vector", lambda e, o=o: e.tensor_scalar(
                    G[:, o + 128:o + 192], G[:, o + 64:o + 128], G[:, o + 195:o + 196],
                    -2e9, ALU.is_ge, ALU.mult), waits=[t_m], inc=self.VE)
                t_w = E("vector", lambda e, o=o: e.tensor_add(
                    G[:, o + 64:o + 128], G[:, o + 64:o + 128], G[:, o + 128:o + 192]),
                    waits=[t_k], inc=self.VE)
            t_v6 = E("vector", lambda e, o=o: e.reduce_max(
                G[:, o + 195:o + 196], G[:, o + 64:o + 128], axis=AX.X),
                waits=[t_w], inc=self.VE)
            t_ind = E("vector", lambda e, o=o: e.tensor_scalar(
                G[:, o + 128:o + 192], G[:, o:o + 64], G[:, o + 195:o + 196], None, ALU.is_ge),
                waits=[t_v6], inc=self.VE)
            t_wd = E("vector", lambda e, o=o: e.tensor_mul(
                G[:, o + 64:o + 128], G[:, o + 128:o + 192], G[:, o:o + 64]),
                waits=[t_ind], inc=self.VE)
            pw = self.ps[0]
            self.mm(pw, [lambda e, o=o, pw=pw: e.transpose(pw.t[0:64, 0:128], G[:, o + 64:o + 128],
                                                           self.CF32[:, 0:128])], waits=[t_wd])
            t_wdc[tt] = self.rd(pw, "vector", lambda e, tt=tt, pw=pw: e.tensor_copy(
                self.WDC[:, tt * 128:(tt + 1) * 128], pw.t[0:64, 0:128]))
            self.mm(pb_f, [lambda e, tt=tt, pb_f=pb_f: e.matmul(
                pb_f.t[:, 0:64], self.WDC[:, tt * 128:(tt + 1) * 128], self.PBF[:],
                start=True, stop=True)], waits=[t_wdc[tt], self.t_pbf])
            t_z1 = self.rd(pb_f, "vector", lambda e, o=o, pb_f=pb_f: e.tensor_scalar(
                G[:, o + 128:o + 192], pb_f.t[:, 0:64], 0.0, -1e9, ALU.is_equal, ALU.mult),
                final=False, inc=self.VE)
            t_z2 = self.rd(pb_f, "vector", lambda e, o=o, pb_f=pb_f: e.tensor_add(
                G[:, o + 128:o + 192], G[:, o + 128:o + 192], pb_f.t[:, 0:64]),
                waits=[t_z1], final=True)
            z3 = G[:, o + 128:o + 192].rearrange("p (g k) -> p g k", g=8)
            gm3 = G[:, o + 224:o + 232].rearrange("p (g k) -> p g k", k=1)
            t_gm = E("vector", lambda e, z3=z3, o=o: e.reduce_max(
                G[:, o + 224:o + 232], z3, axis=AX.X, negate=True), waits=[t_z2], inc=self.VE)

            def f_zc(e, z3=z3, gm3=gm3):
                a, b = broadcast_tensor_aps(z3, gm3)
                return e.tensor_add(a, a, b)
            t_zc = E("vector", f_zc, waits=[t_gm], inc=self.VE)
            t_ez = E("scalar", lambda e, o=o: e.activation(G[:, o + 128:o + 192],
                                                           G[:, o + 128:o + 192], AF.Exp),
                     waits=[t_zc], inc=self.AC)
            t_gs = E("vector", lambda e, z3=z3, o=o: e.reduce_sum(
                G[:, o + 232:o + 240], z3, axis=AX.X), waits=[t_ez], inc=self.VE)
            t_gr = E("vector", lambda e, o=o: e.reciprocal(G[:, o + 240:o + 248],
                                                           G[:, o + 232:o + 240]),
                     waits=[t_gs], inc=self.VE)
            gr3 = G[:, o + 240:o + 248].rearrange("p (g k) -> p g k", k=1)

            def f_sf(e, z3=z3, gr3=gr3):
                a, b = broadcast_tensor_aps(z3, gr3)
                return e.tensor_mul(a, a, b)
            t_sf = E("vector", f_sf, waits=[t_gr], inc=self.VE)
            pw2 = self.ps[1]
            self.mm(pw2, [lambda e, o=o, pw2=pw2: e.transpose(
                pw2.t[0:64, 0:128], G[:, o + 128:o + 192], self.CF32[:, 0:128])],
                waits=[t_sf])
            t_sfc[tt] = self.rd(pw2, "vector", lambda e, tt=tt, pw2=pw2: e.tensor_copy(
                self.SOFTC[:, tt * 128:(tt + 1) * 128], pw2.t[0:64, 0:128]))
            prev[tt % 2] = t_sfc[tt]
        self.t_wdc = tokmax(*t_wdc)
        self.t_sfc = tokmax(*t_sfc)

    def expert(self, f, t_free, t_mu_prev, t_y_prev):
        E = self.emit
        p2, p3 = self.ps[2], self.ps[3]
        self.mm(p2, [
            (lambda e, n0=n0: e.matmul(p2.t[0:64, n0:n0 + SUB],
                                       self.CBF[0:64, 128 + f * 128:128 + f * 128 + 64],
                                       self.SOFTC[0:64, n0:n0 + SUB], start=True, stop=True))
            for n0 in (0, SUB)], waits=[*self.t_sfc, *self.t_consts])
        t_w128 = self.rd(p2, "vector", lambda e: e.tensor_copy(
            self.ARA[0:64, 27648:28672], p2.t[0:64, 0:TW]), waits=tokmax(*(t_free or [])))
        self.mm(p3, [
            (lambda e, n0=n0: e.matmul(p3.t[0:1, n0:n0 + SUB], self.PGBF[0:64, f:f + 1],
                                       self.WDC[0:64, n0:n0 + SUB], start=True, stop=True))
            for n0 in (0, SUB)], waits=[*self.t_wdc, self.t_pg])
        t_scb = self.rd(p3, "vector", lambda e: e.tensor_copy(
            self.QROW[0:1, 1024:2048], p3.t[0:1, 0:TW]))
        self.mm(p3, [
            (lambda e, n0=n0: e.matmul(p3.t[:, n0:n0 + SUB], self.CBF[0:1, 1160:1288],
                                       self.QROW[0:1, 1024 + n0:1024 + n0 + SUB],
                                       start=True, stop=True))
            for n0 in (0, SUB)], waits=[t_scb])
        t_scbc = self.rd(p3, "vector", lambda e: e.tensor_copy(self.SCB[:], p3.t[:, 0:TW]))
        t_mz = E("vector", lambda e: e.memset(self.MUT[:], 0.0),
                 waits=tokmax(t_mu_prev), inc=self.VE)
        _, _, _, t_mu = self.ws_next("mu%d" % f, dest=self.MUT, dest_part=f * 8, waits=[t_mz], sem=self.MU)
        t_xe = [None] * KB_D
        for m in range(KB_D):
            pb = self.ps[m % 2]
            self.mm(pb, [
                (lambda e, n0=n0, m=m, pb=pb: e.matmul(
                    pb.t[:, n0:n0 + SUB], self.MUT[:, m * 128:m * 128 + 128],
                    self.SOFTC[0:64, n0:n0 + SUB], start=True, stop=True))
                for n0 in (0, SUB)], waits=[t_mu, *self.t_sfc])
            t_xe[m] = self.rd(pb, "vector", lambda e, m=m, pb=pb: e.tensor_add(
                self.ARA[:, self.c_xe(m):self.c_xe(m) + TW], pb.t[:, 0:TW],
                self.RES[:, m * TW:(m + 1) * TW]),
                waits=tokmax(*(t_free or [])) if m == 0 else ())
        t_xe_all = tokmax(*t_xe)
        base, kb, mc, t_ld = self.ws_next("qa%d" % f)
        fg = []
        fu = []
        for k in range(KB_D):
            for n0 in (0, SUB):
                def fq(e, k=k, n0=n0, base=base):
                    return e.matmul(p2.t[0:64, n0:n0 + SUB],
                                    self.RING[:, base + k * 128:base + k * 128 + 64],
                                    self.ARA[:, self.c_xe(k) + n0:self.c_xe(k) + n0 + SUB],
                                    start=(k == 0), stop=(k == KB_D - 1))
                fg.append(fq)

                def fq2(e, k=k, n0=n0, base=base):
                    return e.matmul(p3.t[0:64, n0:n0 + SUB],
                                    self.RING[:, base + k * 128 + 64:base + k * 128 + 128],
                                    self.ARA[:, self.c_xe(k) + n0:self.c_xe(k) + n0 + SUB],
                                    start=(k == 0), stop=(k == KB_D - 1))
                fu.append(fq2)
        self.mm(p2, fg, waits=[t_ld, *t_xe_all])
        t_qa_grp = self.mm(p3, fu, waits=[t_ld, *t_xe_all])
        self.ring_free(t_qa_grp)
        t_aw_g = self.rd(p2, "vector", lambda e: e.tensor_mul(
            self.AW[0:64, 0:TW], p2.t[0:64, 0:TW], self.ARA[0:64, 27648:28672]),
            waits=[t_w128])
        t_aw_u = self.rd(p3, "vector", lambda e: e.tensor_mul(
            self.AW[0:64, TW:2 * TW], p3.t[0:64, 0:TW], self.ARA[0:64, 27648:28672]),
            waits=[t_w128])
        t_aw = tokmax(t_aw_g, t_aw_u)
        t_qbg = self.qb_scale("qbg%d" % f, self.QB, II, self.d_gsc, f, t_free)
        t_gate = self.ffn_big("gW%d" % f, KB_D, KB_I, self.QB, t_qbg, t_aw, 0,
                              evac="silu", xe_all=t_xe_all, region_free=t_free)
        t_qbu = self.qb_scale("qbu%d" % f, self.QB, II, self.d_usc, f, t_gate)
        t_gu = self.ffn_big("uW%d" % f, KB_D, KB_I, self.QB, t_qbu, t_aw, TW,
                            evac="gu", xe_all=t_xe_all, gate_toks=t_gate, t_scbc=t_scbc)
        t_gu_all = tokmax(*t_gu)
        base, kb, mc, t_ld = self.ws_next("dqa%d" % f)
        fns = []
        for k in range(KB_I):
            for n0 in (0, SUB):
                def fd(e, k=k, n0=n0, base=base):
                    return e.matmul(p2.t[0:64, n0:n0 + SUB],
                                    self.RING[:, base + k * 64:base + k * 64 + 64],
                                    self.ARA[:, self.c_gate(k) + n0:self.c_gate(k) + n0 + SUB],
                                    start=(k == 0), stop=(k == KB_I - 1))
                fns.append(fd)
        t_dqa_grp = self.mm(p2, fns, waits=[t_ld, *t_gu_all])
        self.ring_free(t_dqa_grp)
        t_awd = self.rd(p2, "vector", lambda e: e.tensor_mul(
            self.AW[0:64, 0:TW], p2.t[0:64, 0:TW], self.ARA[0:64, 27648:28672]),
            waits=[t_w128, *t_aw])
        t_qbd = self.qb_scale("qbd%d" % f, self.QB, D, self.d_dsc, f, t_gu)
        t_y = self.ffn_down("dW%d" % f, f, t_qbd, t_awd, t_gu_all, t_y_prev)
        return t_xe_all, t_y

    def qb_scale(self, tag, dst, C, d_src, f, t_free):
        E = self.emit
        p3 = self.ps[3]
        _, _, _, t_ld = self.ws_next(tag, dest=dst, waits=tokmax(*(t_free or [])), sem=self.QBS)
        t_g = E("gpsimd", lambda e: e.dma_start(out=self.QROW[0:1, 0:C],
                                                in_=d_src[f:f + 1, :]),
                waits=[(p3.rdy, p3.w)], inc=self.QBS, inc_by=16)
        toks = []
        for c0 in range(0, C, TW):
            cw = min(TW, C - c0)
            fns = []
            for n0 in range(0, cw, SUB):
                nw = min(SUB, cw - n0)

                def fb(e, c0=c0, n0=n0, nw=nw):
                    return e.matmul(p3.t[0:64, n0:n0 + nw], self.CBF[0:1, 1160:1224],
                                    self.QROW[0:1, c0 + n0:c0 + n0 + nw], start=True, stop=True)
                fns.append(fb)
            self.mm(p3, fns, waits=[t_g])
            toks.append(self.rd(p3, "vector", lambda e, c0=c0, cw=cw, dst=dst: e.tensor_mul(
                dst[0:64, c0:c0 + cw], dst[0:64, c0:c0 + cw], p3.t[0:64, 0:cw]),
                waits=[t_ld, t_g]))
        return tokmax(*toks)

    def ffn_big(self, tag, kbn, mout, qb, t_qb, t_aw, aw_row, evac, xe_all,
                gate_toks=None, t_scbc=None, region_free=None):
        out_toks = [None] * mout
        m = 0
        t_tt_prev = None
        for p in range(mout):
            base, kb, mc, t_ld = self.ws_next(tag)
            t_grp = None
            for mi in range(mc // 128):
                pb = self.ps[m % 2]
                fns = []
                for k in range(kbn):
                    for n0 in (0, SUB):
                        def f1(e, k=k, n0=n0, base=base, mc=mc, mi=mi, pb=pb):
                            o = base + k * mc + mi * 128
                            return e.matmul(pb.t[:, n0:n0 + SUB], self.RING[:, o:o + 128],
                                            self.ARA[:, self.c_xe(k) + n0:self.c_xe(k) + n0 + SUB],
                                            start=(k == 0), stop=False)
                        fns.append(f1)
                for n0 in (0, SUB):
                    def f2(e, n0=n0, m=m, pb=pb, qb=qb, aw_row=aw_row):
                        return e.matmul(pb.t[:, n0:n0 + SUB], qb[0:64, m * 128:m * 128 + 128],
                                        self.AW[0:64, aw_row + n0:aw_row + n0 + SUB],
                                        start=False, stop=True)
                    fns.append(f2)
                t_grp = self.mm(pb, fns, waits=[t_ld, *xe_all, t_qb, *tokmax(t_aw)])
                if evac == "silu":
                    out_toks[m] = self.rd(pb, "scalar", lambda e, m=m, pb=pb: e.activation(
                        self.ARA[:, self.c_gate(m):self.c_gate(m) + TW], pb.t[:, 0:TW], AF.Silu),
                        waits=tokmax(*(region_free or [])) if m == 0 else ())
                else:
                    t_t1 = self.rd(pb, "vector", lambda e, m=m, pb=pb: e.tensor_mul(
                        self.TTB[:], pb.t[:, 0:TW],
                        self.ARA[:, self.c_gate(m):self.c_gate(m) + TW]),
                        waits=tokmax(gate_toks[m], t_tt_prev))
                    t_gu = self.emit("vector", lambda e, m=m: e.tensor_mul(
                        self.ARA[:, self.c_gate(m):self.c_gate(m) + TW], self.TTB[:],
                        self.SCB[:]), waits=[t_t1, t_scbc], inc=self.VE)
                    t_tt_prev = t_gu
                    out_toks[m] = t_gu
                m += 1
            self.ring_free(t_grp)
        return out_toks

    def ffn_down(self, tag, f, t_qbd, t_awd, t_gu_all, t_y_prev):
        t_y = [None] * KB_D
        m = 0
        for p in range((D + 127) // 128):
            base, kb, mc, t_ld = self.ws_next(tag)
            t_grp = None
            for mi in range(mc // 128):
                pb = self.ps[m % 2]
                fns = []
                for k in range(KB_I):
                    for n0 in (0, SUB):
                        def f1(e, k=k, n0=n0, base=base, mc=mc, mi=mi, pb=pb):
                            o = base + k * mc + mi * 128
                            return e.matmul(pb.t[:, n0:n0 + SUB], self.RING[:, o:o + 128],
                                            self.ARA[:, self.c_gate(k) + n0:self.c_gate(k) + n0 + SUB],
                                            start=(k == 0), stop=False)
                        fns.append(f1)
                for n0 in (0, SUB):
                    def f2(e, n0=n0, m=m, pb=pb):
                        return e.matmul(pb.t[:, n0:n0 + SUB], self.QB[0:64, m * 128:m * 128 + 128],
                                        self.AW[0:64, n0:n0 + SUB], start=False, stop=True)
                    fns.append(f2)
                t_grp = self.mm(pb, fns, waits=[t_ld, *t_gu_all, t_qbd, t_awd])
                if f == 0:
                    t_y[m] = self.rd(pb, "vector", lambda e, m=m, pb=pb: e.tensor_copy(
                        self.YS[:, m * TW:(m + 1) * TW], pb.t[:, 0:TW]),
                        waits=tokmax(*(t_y_prev or [])) if m == 0 else ())
                else:
                    t_y[m] = self.rd(pb, "vector", lambda e, m=m, pb=pb: e.tensor_add(
                        self.YS[:, m * TW:(m + 1) * TW], self.YS[:, m * TW:(m + 1) * TW],
                        pb.t[:, 0:TW]),
                        waits=tokmax(*(t_y_prev or [])) if m == 0 else ())
                m += 1
            self.ring_free(t_grp)
        return t_y

    def shared(self, t_free, t_y_prev):
        E = self.emit
        t_shg = [None] * KB_S
        ms = 0
        for p in range((ISH + 127) // 128):
            base, kb, mc, t_ld = self.ws_next("shg")
            t_grp = None
            for mi in range(mc // 128):
                pb = self.ps[ms % 2]
                fns = []
                for k in range(KB_D):
                    for n0 in (0, SUB):
                        def f1(e, k=k, n0=n0, base=base, mc=mc, mi=mi, pb=pb):
                            o = base + k * mc + mi * 128
                            return e.matmul(pb.t[:, n0:n0 + SUB], self.RING[:, o:o + 128],
                                            self.RES[:, k * TW + n0:k * TW + n0 + SUB],
                                            start=(k == 0), stop=(k == KB_D - 1))
                        fns.append(f1)
                t_grp = self.mm(pb, fns, waits=[t_ld, *t_free])
                t_shg[ms] = self.rd(pb, "scalar", lambda e, ms=ms, pb=pb: e.activation(
                    self.ARA[:, self.c_sh(ms):self.c_sh(ms) + TW], pb.t[:, 0:TW], AF.Silu),
                    waits=tokmax(*(t_free if ms < 2 else [])))
                ms += 1
            self.ring_free(t_grp)
        t_prod = [None] * KB_S
        ms = 0
        for p in range((ISH + 127) // 128):
            base, kb, mc, t_ld = self.ws_next("shu")
            t_grp = None
            for mi in range(mc // 128):
                pb = self.ps[ms % 2]
                fns = []
                for k in range(KB_D):
                    for n0 in (0, SUB):
                        def f1(e, k=k, n0=n0, base=base, mc=mc, mi=mi, pb=pb):
                            o = base + k * mc + mi * 128
                            return e.matmul(pb.t[:, n0:n0 + SUB], self.RING[:, o:o + 128],
                                            self.RES[:, k * TW + n0:k * TW + n0 + SUB],
                                            start=(k == 0), stop=(k == KB_D - 1))
                        fns.append(f1)
                t_grp = self.mm(pb, fns, waits=[t_ld])
                t_prod[ms] = self.rd(pb, "vector", lambda e, ms=ms, pb=pb: e.tensor_mul(
                    self.ARA[:, self.c_sh(ms):self.c_sh(ms) + TW],
                    self.ARA[:, self.c_sh(ms):self.c_sh(ms) + TW], pb.t[:, 0:TW]),
                    waits=[t_shg[ms]])
                ms += 1
            self.ring_free(t_grp)
        t_prod_all = tokmax(*t_prod)
        t_y = [None] * KB_D
        m = 0
        for p in range((D + 127) // 128):
            base, kb, mc, t_ld = self.ws_next("shd")
            t_grp = None
            for mi in range(mc // 128):
                pb = self.ps[m % 2]
                fns = []
                for k in range(KB_S):
                    for n0 in (0, SUB):
                        def f1(e, k=k, n0=n0, base=base, mc=mc, mi=mi, pb=pb):
                            o = base + k * mc + mi * 128
                            return e.matmul(pb.t[:, n0:n0 + SUB], self.RING[:, o:o + 128],
                                            self.ARA[:, self.c_sh(k) + n0:self.c_sh(k) + n0 + SUB],
                                            start=(k == 0), stop=(k == KB_S - 1))
                        fns.append(f1)
                t_grp = self.mm(pb, fns, waits=[t_ld, *t_prod_all])
                t_y[m] = self.rd(pb, "vector", lambda e, m=m, pb=pb: e.tensor_add(
                    self.YS[:, m * TW:(m + 1) * TW], self.YS[:, m * TW:(m + 1) * TW],
                    pb.t[:, 0:TW]),
                    waits=tokmax(*(t_y_prev or [])) if m == 0 else ())
                m += 1
            self.ring_free(t_grp)
        return t_y

    def final(self, t_y):
        E = self.emit
        t_yall = tokmax(*t_y)
        s1, s2 = self.ps[2], self.ps[3]
        r2_toks = []
        t_sq_prev = None
        pn_toks = []
        s1.w += 1
        s2.w += 1
        for m in range(KB_D):
            t_r2 = E("vector", lambda e, m=m: e.tensor_add(
                self.ARA[:, m * TW:(m + 1) * TW], self.RES[:, m * TW:(m + 1) * TW],
                self.YS[:, m * TW:(m + 1) * TW]),
                waits=t_yall if m == 0 else [], inc=self.VE)
            r2_toks.append(t_r2)
            sq = self.TTB if m % 2 == 0 else self.SCB
            t_sq = E("scalar", lambda e, m=m, sq=sq: e.activation(
                sq[:], self.ARA[:, m * TW:(m + 1) * TW], AF.Square),
                waits=tokmax(t_r2, t_sq_prev, pn_toks[m - 2] if m >= 2 else None),
                inc=self.AC)
            t_sq_prev = t_sq
            first = m == 0
            last = m == KB_D - 1
            for n0 in (0, SUB):
                self.emit("tensor", lambda e, m=m, n0=n0, first=first, last=last: e.matmul(
                    s1.t[0:1, n0:n0 + SUB], self.CBF[:, 1160:1161],
                    self.ARA[:, m * TW + n0:m * TW + n0 + SUB], start=first, stop=last),
                    waits=tokmax(t_r2, (s1.done, s1.r) if first and n0 == 0 else None),
                    inc=s1.rdy if (last and n0 == SUB) else None)
                inc_s2 = (s2.rdy if last else self.PN) if n0 == SUB else None
                tk2 = self.emit("tensor", lambda e, n0=n0, sq=sq, first=first, last=last: e.matmul(
                    s2.t[0:1, n0:n0 + SUB], self.CBF[:, 1160:1161],
                    sq[:, n0:n0 + SUB], start=first, stop=last),
                    waits=tokmax(t_sq, (s2.done, s2.r) if first and n0 == 0 else None),
                    inc=inc_s2)
                if n0 == SUB:
                    pn_toks.append(tk2)
        out_dma = [None] * KB_D

        def dst_fn(m):
            return self.YS[:, (m % 2) * TW:(m % 2) * TW + TW]

        def dst_wait(m):
            return (self.OUTS[m % 2], 16 * (m // 2)) if m >= 2 else None

        x2_toks = self.ln_norm(lambda m: self.ARA[:, m * TW:(m + 1) * TW],
                               dst_fn, r2_toks, dst_wait_fn=dst_wait)
        for m in range(KB_D):
            E("sync", lambda e, m=m: e.dma_start(
                out=self.d_out[m * 128:(m + 1) * 128, :],
                in_=self.YS[:, (m % 2) * TW:(m % 2) * TW + TW]),
                waits=[x2_toks[m]], inc=self.OUTS[m % 2], inc_by=16)
        E("sync", lambda e: e.nop(),
          waits=[(self.OUTS[0], 16 * (KB_D // 2)), (self.OUTS[1], 16 * (KB_D // 2))])

    def emit_taps(self, tap_srcs):
        allw = [(self.VE, self.VE.n), (self.AC, self.AC.n), (self.GD, self.GD.n)]
        allw += [(pb.rdy, pb.w) for pb in self.ps]
        allw += [(pb.done, pb.r) for pb in self.ps]
        n = 0
        for name, src in tap_srcs.items():
            self.emit("gpsimd", lambda e, name=name, src=src: e.dma_start(
                out=self.tap_outs[name], in_=src), waits=allw, inc=self.OUT, inc_by=16)
            n += 1
        if n:
            self.emit("sync", lambda e: e.nop(),
                      waits=[(self.OUT, self.OUT.n)])

    def phases(self, upto="all"):
        self.prep()
        t_xs = self.load_xs(0, [])
        t_state_tok = self.t_memset[-1] if self.t_memset else None
        t_aux = list(self.t_memset)
        resid_all = []
        for sub in range(NSUB):
            u_toks, t_a1_grp = self.a1(sub, t_xs, t_aux)
            right_toks, t_chain = self.a34(sub, u_toks, t_state_tok)
            if sub == 1:
                t_state_tok = self.mask_state(t_chain)
            else:
                t_state_tok = t_chain
            if sub >= 2:
                lr_toks = self.a2(sub, right_toks, t_xs)
                r_toks, t_pe_stats = self.a5(sub, lr_toks, t_xs, t_chain)
                resid_all.extend(r_toks)
                t_xs_dep = tokmax(*r_toks)
                t_aux = [t_state_tok, t_pe_stats]
            else:
                t_xs_dep = tokmax(t_a1_grp)
                t_aux = [t_state_tok]
            if sub < NSUB - 1:
                t_xs = self.load_xs(sub + 1, t_xs_dep)
        if upto == "attn":
            return {"resid": resid_all}
        x1_toks = self.ln_norm(lambda m: self.RES[:, m * TW:(m + 1) * TW],
                               lambda m: self.RES[:, m * TW:(m + 1) * TW], resid_all)
        if upto == "ln1":
            return {"x1": x1_toks}
        self.gating(x1_toks)
        if upto == "gating":
            return {"x1": x1_toks}
        t_free = x1_toks
        t_mu_prev = None
        t_y = None
        for f in range(NF):
            t_xe_all, t_y_new = self.expert(f, t_free, t_mu_prev, t_y)
            t_mu_prev = t_xe_all
            t_free = t_y_new
            t_y = t_y_new
        if upto == "experts":
            return {"y": t_y}
        t_ysh = self.shared(t_y, t_y)
        if upto == "shared":
            return {"y": t_ysh}
        self.final(t_ysh)
        return {}

    def build_block(self):
        with self.nc.Block() as blk:
            for eng in self.ENGS:
                ops = self.ops[eng]
                if not ops:
                    continue
                getattr(blk, eng)(lambda e, ops=ops: [op(e) for op in ops])
        return self.nc


def build_kernel(taps=(), upto="all", tap_src_fn=None):
    k = K(taps)
    k.declare()
    cx = ExitStack()
    with cx:
        k.alloc(cx)
        k.phases(upto)
        assert len(k.ring_marks) >= k.ring_piece - 2, (len(k.ring_marks), k.ring_piece)
        if tap_src_fn:
            k.emit_taps(tap_src_fn(k))
        k.build_block()
    return k



_CACHE = {}


def _host_inputs(inputs):
    plan = wstream_plan(inputs)
    ws = np.concatenate([a.ravel() for (_, _, _, a) in plan]).astype(np.float32)
    vecs = host_vecs(inputs)
    cbf = host_cbf()
    cf32 = host_cf32()
    inv = np.ascontiguousarray(np.asarray(inputs["L0_ffn_inv_mapping"], np.int32).reshape(1, 64))
    gsc = np.ascontiguousarray(np.asarray(inputs["L0_ffn_gate_scale"], np.float32))
    usc = np.ascontiguousarray(np.asarray(inputs["L0_ffn_up_scale"], np.float32))
    dsc = np.ascontiguousarray(np.asarray(inputs["L0_ffn_down_scale"], np.float32))
    x = np.asarray(inputs["L0_attn_x"], np.float32)
    maps = []
    for c in range(8):
        b, h = c // 2, c % 2
        xt = np.ascontiguousarray(x[b].T)
        if h == 0:
            xt = np.ascontiguousarray(np.concatenate([xt[:, :TW], xt[:, :TW]], axis=1))
        hm = np.full((128, 1), float(h), np.float32)
        maps.append({"wstream": ws, "xt": xt, "vecs": vecs, "cbf": cbf, "cf32": cf32,
                     "invm": inv, "hmask": hm, "gsc": gsc, "usc": usc, "dsc": dsc})
    return maps


def kernel(**inputs):
    if "nc" not in _CACHE:
        _CACHE["nc"] = build_kernel().nc
    nc = _CACHE["nc"]
    maps = _host_inputs(inputs)
    res = run_bass_kernel_spmd(nc, maps, core_ids=list(range(8)))
    out = np.empty((4, 2048, D), np.float32)
    for c in range(8):
        b, h = c // 2, c % 2
        out[b, h * TW:(h + 1) * TW, :] = res.results[c]["out"].T
    return out
